# version 03
# baseline (speedup 1.0000x reference)
import numpy as np
from contextlib import ExitStack
import concourse.bass as bass
import concourse.mybir as mybir
from concourse.bass_utils import run_bass_kernel_spmd

F32 = mybir.dt.float32
BF16 = mybir.dt.bfloat16
F16 = mybir.dt.float16
I32 = mybir.dt.int32
I8 = mybir.dt.int8
AF = mybir.ActivationFunctionType
ALU = mybir.AluOpType
AX = mybir.AxisListType.X

T = 2048
D = 1024
NC = 8
NT = 16
QG = 512
NQG = T // QG
N_LAYERS = 4
CORES = list(range(8))
L1_ATTN = True
L1_FFN = True
L2_ATTN = True
L2_FFN = True
L2FFN_PH = 5
L2A_N = 24
L2A_COLL = True
L2D_N = 8
L2D_MM = True
L3_ATTN = True
L3_PH = 4
L3M = 5
L3G = 5
DEBUG_TAPS = ()
DBG_REGION = None

ENGINES = ("sync", "tensor", "vector", "scalar", "gpsimd")
ARENA_BYTES = 49152


class Sem:
    __slots__ = ("h", "v", "name")

    def __init__(self, h, name):
        self.h = h
        self.v = 0
        self.name = name


class P:
    def __init__(self, nc, st):
        self.nc = nc
        self.st = st
        self.q = {e: [] for e in ENGINES}
        self.sems = {}

    def sem(self, name):
        if name not in self.sems:
            h = self.st.enter_context(self.nc.semaphore(name))
            self.sems[name] = Sem(h, name)
        return self.sems[name]

    def emit(self, eng, fn):
        self.q[eng].append(fn)

    def wait(self, eng, sem, val=None):
        v = sem.v if val is None else val
        if v > 0:
            self.emit(eng, lambda e, s=sem.h, v=v: e.wait_ge(s, v))

    def run(self, eng, fn, inc=None, inc_n=1):
        if inc is not None:
            inc.v += inc_n
            self.emit(eng, lambda e, f=fn, s=inc.h, n=inc_n: f(e).then_inc(s, n))
        else:
            self.emit(eng, lambda e, f=fn: f(e))

    def dma(self, eng, out, in_, sem):
        self.run(eng, lambda e, o=out, i=in_: e.dma_start(o, i), inc=sem, inc_n=16)

    def realize(self, block):
        for eng in ENGINES:
            def section(e, q=self.q[eng]):
                for f in q:
                    f(e)
            getattr(block, eng)(section)


def ts(i, n):
    return slice(i * n, (i + 1) * n)


def rs(a, b):
    return slice(a, b)


def build_nc():
    nc = bass.Bass("TRN2", target_bir_lowering=False,
                   detect_race_conditions=False)
    st = ExitStack()
    p = P(nc, st)
    E = st.enter_context

    dram = {}

    def din(name, shape, dt=F32):
        dram[name] = nc.dram_tensor(name, shape, dt, kind="ExternalInput").ap()
        return dram[name]

    din("xT", [D, T])
    din("L0_WqA", [D, D + 1])
    din("L0_WkA", [D, D + 1])
    din("L0_W", [D, D])
    din("L0_A1", [D, D])
    din("L0_A2", [D, D])
    din("L0_bias", [128, 8])
    din("L0_bqc", [128, 8])
    din("L1_WattnT", [D, 3 * D])
    din("L1_WprojT", [D, D])
    din("L1_wgT", [D, 4096])
    din("L1_wuT", [D, 4096])
    din("L1_wdT", [4096, D])
    din("IOTA_T", [128, T])
    din("L1_ridx", [128, 8])
    dram["L1_wq0"] = nc.dram_tensor("L1_wq0", [D, 4096], BF16).ap()
    dram["L1_wq1"] = nc.dram_tensor("L1_wq1", [D, 4096], BF16).ap()
    dram["L1_wq2"] = nc.dram_tensor("L1_wq2", [4096, D], BF16).ap()
    din("L2_Wq", [D, D])
    din("L2_Wk", [D, D])
    din("L2_Wv", [D, D])
    din("L2_Wo", [D, D])
    din("L2_Wdkv", [D, 256])
    din("L2_Wupk", [256, D])
    din("L2_Wupv", [256, D])
    din("L2_wgT", [D, 3072])
    din("L2_wuT", [D, 3072])
    din("L2_wdT", [3072, D])
    din("L2_wgN", [3072, D])
    din("L2_wuN", [3072, D])
    din("L2_wdN", [D, 3072])
    din("BDJ", [128, 3072])
    dram["L2_wq0"] = nc.dram_tensor("L2_wq0", [D, 3072], BF16).ap()
    dram["L2_wq1"] = nc.dram_tensor("L2_wq1", [D, 3072], BF16).ap()
    dram["L2_wq2s"] = nc.dram_tensor("L2_wq2s", [3072, D], BF16).ap()
    dram["L2_upraw"] = nc.dram_tensor("L2_upraw", [3072, T], BF16).ap()
    dram["L2_gtraw"] = nc.dram_tensor("L2_gtraw", [3072, T], BF16).ap()
    dram["L2_t3"] = nc.dram_tensor("L2_t3", [3072, T], BF16).ap()
    NCO = len(CORES)
    din("L3_WqT", [D, 64])
    din("L3_WkT", [D, 64])
    din("L3_WvT", [D, 64])
    din("L3_bqkv", [64, 3])
    din("L3_sw1T", [D, 512])
    din("L3_sw2T", [D, 512])
    din("L3_sw3T", [512, D])
    din("L3_scols", [128, 24])
    din("L3_sgwT", [128, 8])
    din("L3_egwT", [D, 8])
    din("L3_gb", [128, 3])
    din("L3_ew1T", [8 * D, 512])
    din("L3_ew2T", [8 * D, 512])
    din("L3_ew3T", [8 * 512, D])
    din("L3_ebcols", [128, 128])
    din("L3_selcols", [128, 8])
    din("Gc", [T, T], BF16)
    din("G2", [T, T], BF16)
    ag_kv_in = nc.dram_tensor("ag_kv_in", [128, T], BF16)
    ag_kv_out = nc.dram_tensor("ag_kv_out", [128 * NCO, T], BF16)
    xm_in = nc.dram_tensor("xm_in", [T, D], BF16)
    xm_out = nc.dram_tensor("xm_out", [T // NCO, D], BF16)
    rp_in = nc.dram_tensor("rp_in", [T // NCO, D], BF16)
    rp_out = nc.dram_tensor("rp_out", [T, D], BF16)
    dram["ag_kv_in"] = ag_kv_in.ap()
    dram["ag_kv_out"] = ag_kv_out.ap()
    dram["xm_in"] = xm_in.ap()
    dram["xm_out"] = xm_out.ap()
    dram["rp_in"] = rp_in.ap()
    dram["rp_out"] = rp_out.ap()
    cc_in_t = nc.dram_tensor("cc_in", [1, 8], F32)
    cc_out_t = nc.dram_tensor("cc_out", [1, 8], F32)
    dram["cc_in"] = cc_in_t.ap()
    dram["cc_out"] = cc_out_t.ap()
    out_ap = nc.dram_tensor("out", [D, T], F32, kind="ExternalOutput").ap()

    x_sb = E(nc.sbuf_tensor("x_sb", [128, NC * T], F32))
    xn_bf = E(nc.sbuf_tensor("xn_bf", [128, NC * T], BF16))
    xn_tok = E(nc.sbuf_tensor("xn_tok", [128, NT * D], BF16))
    identb = E(nc.sbuf_tensor("identb", [128, 128], BF16))
    identf = E(nc.sbuf_tensor("identf", [128, 128], F32))
    identh = E(nc.sbuf_tensor("identh", [128, 128], F16))
    ones_col = E(nc.sbuf_tensor("ones_col", [128, 1], BF16))
    ones_row = E(nc.sbuf_tensor("ones_row", [1, 128], BF16))
    e9 = E(nc.sbuf_tensor("e9", [128, 512], BF16))
    consts = E(nc.sbuf_tensor("consts", [128, 12], F32))
    stat = E(nc.sbuf_tensor("stat", [128, 32], F32))
    rows = E(nc.sbuf_tensor("rows", [1, 1024], F32))
    rows_bf = E(nc.sbuf_tensor("rows_bf", [1, 1026], BF16))
    arena = E(nc.sbuf_tensor("arena", [128, 36864], I8))
    arena2 = E(nc.sbuf_tensor("arena2", [128, 36192], I8))
    pb = [E(nc.psum_tensor(f"pb{i}", [128, 512], F32)) for i in range(8)]

    def ab(which, off, dt, n):
        sz = {F32: 4, BF16: 2, F16: 2, I32: 4}[dt]
        return which[:, off:off + n * sz].bitcast(dt)

    sW = p.sem("sW")
    sG = p.sem("sG")
    sT = p.sem("sT")
    sV = p.sem("sV")
    sS = p.sem("sS")
    sE = p.sem("sE")


    def chunked_load(eng, sbuf_ap, dram_ap, ncols, sem):
        nchunk = dram_ap.shape[0] // 128
        src = dram_ap.rearrange("(c p) o -> p c o", p=128)
        dst = sbuf_ap.rearrange("p (c o) -> p c o", c=nchunk)
        p.dma(eng, dst, src, sem)

    def tap(name, ap_, shape, dt=F32):
        if name not in DEBUG_TAPS:
            return
        o = nc.dram_tensor("tap_" + name, shape, dt, kind="ExternalOutput").ap()
        p.run("vector", lambda e: e.nop(), inc=sV)
        p.run("scalar", lambda e: e.nop(), inc=sS)
        p.run("tensor", lambda e: e.nop(), inc=sT)
        for eng in ("sync",):
            p.wait(eng, sV)
            p.wait(eng, sS)
            p.wait(eng, sT)
            p.wait(eng, sG)
        p.dma("sync", o, ap_, sW)
        for eng in ("tensor", "vector", "scalar", "gpsimd"):
            p.wait(eng, sW)

    for j in range(NC):
        p.dma("sync", x_sb[:, ts(j, T)], dram["xT"][ts(j, 128), :], sW)
    x_loaded = sW.v

    p.run("gpsimd", lambda e: e.iota(stat[:, 0:1], pattern=[[0, 1]], base=0,
                                     channel_multiplier=1,
                                     allow_small_or_imprecise_dtypes=True))
    p.run("gpsimd", lambda e: e.memset(e9[:], 0.0), inc=sE)
    p.wait("gpsimd", sE)
    p.run("gpsimd", lambda e: e.memset(e9[0:1, :], 1.0))
    p.run("gpsimd", lambda e: e.memset(ones_col[:], 1.0))
    p.run("gpsimd", lambda e: e.memset(ones_row[:], 1.0))
    p.run("gpsimd", lambda e: e.memset(consts[:, 0:1], 0.125))
    p.run("gpsimd", lambda e: e.memset(consts[:, 1:2], -0.125))
    p.run("gpsimd", lambda e: e.memset(consts[:, 2:3], 1.0 / 1024.0))
    p.run("gpsimd", lambda e: e.memset(consts[:, 3:4], 1e-5))
    p.run("gpsimd", lambda e: e.memset(consts[:, 4:5], 2.0 / T))
    p.run("gpsimd", lambda e: e.memset(consts[:, 5:6], 1.0 / 32.0))
    p.run("gpsimd", lambda e: e.memset(consts[:, 6:7], -1.0 / 32.0))
    p.run("gpsimd", lambda e: e.memset(consts[:, 7:8], 1e-4))
    p.run("gpsimd", lambda e: e.memset(consts[:, 8:9], -1e-4))
    p.run("gpsimd", lambda e: e.memset(consts[:, 9:10], 1.0 / 512.0))
    p.wait("gpsimd", sE)
    p.dma("gpsimd", dram["cc_in"][0:1, 0:8], e9[1:2, 0:8], sG)
    iota_row = ab(arena, 0, F32, 128)
    p.run("gpsimd", lambda e: e.iota(iota_row[:, 0:128], pattern=[[1, 128]],
                                     base=0, channel_multiplier=0,
                                     allow_small_or_imprecise_dtypes=True),
          inc=sE)
    p.wait("vector", sE)
    p.run("vector", lambda e: e.tensor_scalar(identf[:], iota_row[:, 0:128],
                                              stat[:, 0:1], None, ALU.is_equal))
    p.run("vector", lambda e: e.tensor_copy(identb[:], identf[:]))
    p.run("vector", lambda e: e.tensor_copy(identh[:], identf[:]), inc=sV)
    init_v = sV.v
    p.wait("tensor", sG)
    p.wait("tensor", sE)
    p.wait("tensor", sV, init_v)
    p.wait("tensor", sW, x_loaded)
    p.wait("scalar", sG)
    p.wait("scalar", sE)
    p.wait("scalar", sW, x_loaded)
    p.wait("vector", sW, x_loaded)

    def layernorm(do_tok=True):
        xbf = ab(arena2, 0, BF16, 4 * QG)
        sq = ab(arena2, 4 * QG * 2, BF16, 4 * QG)
        mrow = rows[0:1, 0:512]
        vrow = rows[0:1, 512:1024]
        r_bf = rows_bf[0:1, 0:512]
        b_bf = rows_bf[0:1, 512:1024]
        t1 = ab(arena2, 16 * QG, F32, QG)
        for g in range(NQG):
            sv0 = sV.v
            ss0 = sS.v
            for j in range(NC):
                pp = j % 2
                if j >= 2:
                    p.wait("vector", sT)
                    p.wait("scalar", sT)
                p.run("vector", lambda e, j=j, g=g, pp=pp: e.tensor_copy(
                    xbf[:, ts(pp, QG)],
                    x_sb[:, j * T + g * QG: j * T + g * QG + QG]), inc=sV)
                p.run("scalar", lambda e, j=j, g=g, pp=pp: e.activation(
                    sq[:, ts(pp, QG)],
                    x_sb[:, j * T + g * QG: j * T + g * QG + QG], AF.Square),
                    inc=sS)
                p.wait("tensor", sV)
                p.wait("tensor", sS)
                p.run("tensor", lambda e, j=j, pp=pp: e.matmul(
                    pb[6][0:1, 0:512], ones_col[:], xbf[:, ts(pp, QG)],
                    start=(j == 0), stop=(j == NC - 1)))
                p.run("tensor", lambda e, j=j, pp=pp: e.matmul(
                    pb[7][0:1, 0:512], ones_col[:], sq[:, ts(pp, QG)],
                    start=(j == 0), stop=(j == NC - 1)),
                    inc=sT)
            p.wait("vector", sT)
            p.run("vector", lambda e: e.tensor_scalar(
                mrow, pb[6][0:1, 0:512], consts[0:1, 2:3], None, ALU.mult))
            p.run("vector", lambda e: e.tensor_scalar(
                vrow, pb[7][0:1, 0:512], consts[0:1, 2:3], None, ALU.mult))
            p.run("vector", lambda e: e.tensor_tensor(t1[0:1, 0:512], mrow, mrow,
                                                      ALU.mult))
            p.run("vector", lambda e: e.tensor_tensor(vrow, vrow, t1[0:1, 0:512],
                                                      ALU.subtract), inc=sV)
            p.wait("scalar", sV)
            p.run("scalar", lambda e: e.activation(vrow, vrow, AF.Sqrt,
                                                   bias=consts[0:1, 3:4]),
                  inc=sS)
            p.wait("vector", sS)
            p.run("vector", lambda e: e.reciprocal(vrow, vrow))
            p.run("vector", lambda e: e.tensor_copy(r_bf, vrow))
            p.run("vector", lambda e: e.tensor_tensor(mrow, mrow, vrow, ALU.mult))
            p.run("vector", lambda e: e.tensor_copy(b_bf, mrow), inc=sV)
            p.wait("tensor", sV)
            p.run("tensor", lambda e: e.matmul(
                pb[4][:, 0:512], ones_row[:], r_bf, start=True, stop=True))
            p.run("tensor", lambda e: e.matmul(
                pb[5][:, 0:512], ones_row[:], b_bf, start=True, stop=True),
                inc=sT)
            p.wait("vector", sT)
            for j in range(NC):
                p.run("vector", lambda e, j=j, g=g: e.tensor_tensor(
                    t1[:, 0:QG], x_sb[:, j * T + g * QG: j * T + g * QG + QG],
                    pb[4][:, 0:512], ALU.mult))
                p.run("vector", lambda e, j=j, g=g: e.tensor_tensor(
                    xn_bf[:, j * T + g * QG: j * T + g * QG + QG],
                    t1[:, 0:QG], pb[5][:, 0:512], ALU.subtract),
                    inc=sV if j == NC - 1 else None)
            p.wait("tensor", sV)
        if do_tok:
            transpose_to_tok(xn_bf, xn_tok)

    def transpose_to_tok(src_bf, dst_tok):
        ev0 = sV.v
        for u in range(NT):
            bankb = pb[u % 2][:].bitcast(BF16)
            if u >= 2:
                p.wait("tensor", sV, ev0 + (u - 2) + 1)
            for j in range(NC):
                p.run("tensor", lambda e, u=u, j=j, bankb=bankb: e.transpose(
                    bankb[:, ts(j, 128)],
                    src_bf[:, j * T + u * 128: j * T + u * 128 + 128],
                    identb[:]), inc=sT if j == NC - 1 else None)
            p.wait("vector", sT)
            p.run("vector", lambda e, u=u, bankb=bankb: e.tensor_copy(
                dst_tok[:, ts(u, D)], bankb[:]), inc=sV)
        p.wait("tensor", sV)

    def eye_add():
        for j in range(NC):
            p.run("vector", lambda e, j=j: e.tensor_tensor(
                x_sb[:, ts(j, T)], x_sb[:, ts(j, T)], xn_bf[:, ts(j, T)],
                ALU.add), inc=sV if j == NC - 1 else None)

    def layer0():
        layernorm()
        DA = D + 1
        wkb = ab(arena2, 0, BF16, NC * DA)
        wqs = [ab(arena, i * 2048, BF16, NC * 128) for i in range(2)]
        mp = ab(arena2, 16416, BF16, 9 * DA)
        p.wait("gpsimd", sT)
        p.wait("gpsimd", sV)
        chunked_load("gpsimd", wkb[:, 0:NC * DA], dram["L0_WkA"], DA, sG)
        p.run("gpsimd", lambda e: e.memset(mp[:, 8 * DA:9 * DA], 0.0), inc=sE)
        p.wait("vector", sE)
        for c in range(9):
            buf = wqs[c % 2]
            if c >= 2:
                p.wait("gpsimd", sT)
            wd = 128 if c < 8 else 1
            if c < 8:
                chunked_load("gpsimd", buf[:, 0:NC * wd],
                             dram["L0_WqA"][:, ts(c, 128)], wd, sG)
            else:
                p.dma("gpsimd", buf[:, 0:NC], dram["L0_bqc"], sG)
            p.wait("tensor", sG)
            p.wait("tensor", sW, x_loaded)
            m_ = 128 if c < 8 else 1
            for ng, (n0, nn) in enumerate(((0, 512), (512, 512), (1024, 1))):
                bank = pb[2 + (ng % 2)]
                for e_ in range(NC):
                    p.run("tensor", lambda e, c=c, ng=ng, e_=e_, n0=n0, nn=nn,
                          bank=bank, m_=m_, buf=buf, wd=wd: e.matmul(
                        bank[0:m_, 0:nn], buf[:, e_ * wd:e_ * wd + wd],
                        wkb[:, e_ * DA + n0:e_ * DA + n0 + nn],
                        start=(e_ == 0), stop=(e_ == NC - 1)),
                        inc=sT if e_ == NC - 1 else None)
                p.wait("vector", sT)
                p.run("vector", lambda e, c=c, n0=n0, nn=nn, m_=m_, bank=bank:
                      e.tensor_copy(mp[0:m_, c * DA + n0:c * DA + n0 + nn],
                                    bank[0:m_, 0:nn]), inc=sV)
            p.wait("tensor", sV)
        ubf = ab(arena, 0, BF16, 9 * T)
        p.run("gpsimd", lambda e: e.memset(ubf[:, 8 * T:9 * T], 0.0), inc=sE)
        p.wait("vector", sE)
        ev_u = sV.v
        it_u = 0
        for c in range(9):
            m_ = 128 if c < 8 else 1
            for g in range(NQG):
                bank = pb[it_u % 2]
                if it_u >= 2:
                    p.wait("tensor", sV, ev_u + it_u - 1)
                for j in range(9):
                    lhsT = mp[:, j * DA + c * 128:j * DA + c * 128 + m_]
                    rhs = (xn_bf[:, j * T + g * QG:j * T + g * QG + QG]
                           if j < 8 else e9[:, 0:512])
                    p.run("tensor", lambda e, lhsT=lhsT, rhs=rhs, m_=m_,
                          bank=bank, j=j: e.matmul(
                        bank[0:m_, 0:512], lhsT, rhs,
                        start=(j == 0), stop=(j == 8)),
                        inc=sT if j == 8 else None)
                p.wait("vector", sT)
                p.run("vector", lambda e, c=c, g=g, m_=m_, bank=bank:
                      e.tensor_copy(ubf[0:m_, c * T + g * QG:c * T + g * QG + QG],
                                    bank[0:m_, 0:512]), inc=sV)
                it_u += 1
        p.wait("tensor", sV)
        for j in range(NC):
            p.run("vector", lambda e, j=j: e.tensor_reduce(
                stat[:, 16 + j:17 + j], xn_bf[:, ts(j, T)], AX, ALU.add))
        p.run("vector", lambda e: e.tensor_scalar(
            stat[:, 8:16], stat[:, 16:24], consts[:, 4:5], None, ALU.mult),
            inc=sV)
        Ssc = ab(arena2, 0, F16, NT * QG)
        Pbf = ab(arena2, 16384, BF16, NT * QG)
        recipb = ab(arena2, 32768, F32, QG)
        rrbf = ab(arena2, 34816, BF16, QG)
        gneg = stat[:, 24:25]
        for g in range(NQG):
            v_at_start = sV.v
            s_at_start = sS.v
            for kk in range(NT):
                bank = pb[kk % 2]
                if kk >= 2:
                    p.wait("tensor", sS, s_at_start + (kk - 2) + 1)
                for j in range(9):
                    p.run("tensor", lambda e, kk=kk, j=j, g=g, bank=bank: e.matmul(
                        bank[:, 0:512],
                        e9[:, 0:128] if j == 8
                        else xn_bf[:, j * T + kk * 128:j * T + kk * 128 + 128],
                        ubf[:, j * T + g * QG:j * T + g * QG + QG] if j < 8
                        else ubf[:, 8 * T + g * QG:8 * T + g * QG + QG],
                        start=(j == 0), stop=(j == 8)),
                        inc=sT if j == 8 else None)
                p.wait("vector", sT)
                p.run("vector", lambda e, kk=kk, bank=bank: e.tensor_reduce(
                    stat[:, 25 + (kk % 2):26 + (kk % 2)], bank[:, 0:512], AX,
                    ALU.max))
                p.run("vector", lambda e, kk=kk: e.tensor_tensor(
                    stat[:, 27:28], stat[:, 25 + (kk % 2):26 + (kk % 2)],
                    stat[:, 27:28] if kk > 0 else stat[:, 25:26], ALU.max),
                    inc=sV)
                p.wait("scalar", sV)
                p.run("scalar", lambda e, kk=kk, bank=bank: e.activation(
                    Ssc[:, ts(kk, QG)], bank[:, 0:512], AF.Identity), inc=sS)
            p.wait("tensor", sS)
            p.run("tensor", lambda e: e.transpose(
                pb[2][0:1, 0:128], stat[:, 27:28], identf[:]), inc=sT)
            p.wait("vector", sT)
            gbf = rows_bf[0:1, 1024:1025]
            p.run("vector", lambda e: e.tensor_reduce(
                stat[0:1, 28:29], pb[2][0:1, 0:128], AX, ALU.max))
            p.run("vector", lambda e: e.tensor_scalar(
                stat[0:1, 29:30], stat[0:1, 28:29], consts[0:1, 6:7], None,
                ALU.mult))
            p.run("vector", lambda e, gbf=gbf: e.tensor_copy(
                gbf, stat[0:1, 29:30]), inc=sV)
            p.wait("tensor", sV)
            p.run("tensor", lambda e, gbf=gbf: e.matmul(
                pb[2][:, 0:1], ones_row[:], gbf, start=True, stop=True),
                inc=sT)
            p.wait("vector", sT)
            p.run("vector", lambda e: e.tensor_copy(gneg, pb[2][:, 0:1]), inc=sV)
            p.wait("scalar", sV)
            p.run("scalar", lambda e: e.activation(
                Pbf[:, 0:NT * QG], Ssc[:, 0:NT * QG], AF.Exp,
                bias=gneg, scale=consts[:, 5:6]), inc=sS)
            p.wait("tensor", sS)
            for kk in range(NT):
                p.run("tensor", lambda e, kk=kk: e.matmul(
                    pb[3][0:1, 0:512], ones_col[:], Pbf[:, ts(kk, QG)],
                    start=(kk == 0), stop=(kk == NT - 1)),
                    inc=sT if kk == NT - 1 else None)
            p.wait("vector", sT)
            p.run("vector", lambda e: e.reciprocal(rows[0:1, 0:512], pb[3][0:1, 0:512]))
            p.run("vector", lambda e: e.tensor_copy(rrbf[0:1, 0:512], rows[0:1, 0:512]),
                  inc=sV)
            p.wait("tensor", sV)
            p.run("tensor", lambda e: e.matmul(
                pb[2][:, 0:512], ones_row[:], rrbf[0:1, 0:512],
                start=True, stop=True), inc=sT)
            p.wait("vector", sT)
            p.run("vector", lambda e: e.tensor_copy(recipb[:, 0:512],
                                                    pb[2][:, 0:512]), inc=sV)
            ev_v = sV.v
            for dc in range(NC):
                bank = pb[4 + (dc % 2)]
                if dc >= 2:
                    p.wait("tensor", sV, ev_v + (dc - 2) + 1)
                for kk in range(NT):
                    p.run("tensor", lambda e, dc=dc, kk=kk, bank=bank: e.matmul(
                        bank[:, 0:512],
                        xn_tok[:, kk * D + dc * 128:kk * D + dc * 128 + 128],
                        Pbf[:, ts(kk, QG)],
                        start=(kk == 0), stop=(kk == NT - 1)),
                        inc=sT if kk == NT - 1 else None)
                p.wait("vector", sT)
                t2 = ab(arena2, 0, F32, QG)
                p.run("vector", lambda e, bank=bank: e.tensor_tensor(
                    t2[:, 0:QG], bank[:, 0:512], recipb[:, 0:512], ALU.mult))
                p.run("vector", lambda e, dc=dc, g=g: e.scalar_tensor_tensor(
                    x_sb[:, dc * T + g * QG:dc * T + g * QG + QG],
                    t2[:, 0:QG], stat[:, 8 + dc:9 + dc],
                    x_sb[:, dc * T + g * QG:dc * T + g * QG + QG],
                    ALU.subtract, ALU.add), inc=sV)
            p.wait("tensor", sV)
        eye_add()
        p.wait("gpsimd", sV)
        p.wait("gpsimd", sS)
        p.dma("gpsimd", stat[:, 0:8], dram["L0_bias"], sG)
        bias_tile = stat[:, 0:8]
        h1 = ab(arena, 0, BF16, NC * T)
        h2 = ab(arena2, 0, BF16, NC * T)
        wsl = [ab(arena2, 32768, BF16, NC * 128)]
        srcs = [xn_bf, h1, h2]
        dsts = [h1, h2, None]
        wnames = ["L0_W", "L0_A1", "L0_A2"]
        th = ab(arena, 0, BF16, NC * T)
        for stg in range(3):
            p.wait("tensor", sV)
            p.wait("tensor", sS)
            if stg == 2:
                p.wait("scalar", sG)
            src = srcs[stg]
            esem = sV if stg < 2 else sS
            ev0 = esem.v
            for oc in range(NC):
                wbuf = wsl[0]
                if oc > 0 or stg > 0:
                    p.wait("gpsimd", sT)
                chunked_load("gpsimd", wbuf[:, 0:NC * 128],
                             dram[wnames[stg]][:, ts(oc, 128)], 128, sG)
                p.wait("tensor", sG)
                for g in range(NQG):
                    it = oc * NQG + g
                    bank = pb[it % 2]
                    if it >= 2:
                        p.wait("tensor", esem, ev0 + it - 1)
                    for j in range(NC):
                        p.run("tensor", lambda e, g=g, j=j, bank=bank,
                              src=src, wbuf=wbuf: e.matmul(
                            bank[:, 0:512],
                            wbuf[:, ts(j, 128)],
                            src[:, j * T + g * QG:j * T + g * QG + QG],
                            start=(j == 0), stop=(j == NC - 1)),
                            inc=sT if j == NC - 1 else None)
                    if stg < 2:
                        p.wait("vector", sT)
                        p.run("vector", lambda e, oc=oc, g=g, bank=bank,
                              stg=stg: e.tensor_copy(
                            dsts[stg][:, oc * T + g * QG:oc * T + g * QG + QG],
                            bank[:, 0:512]), inc=sV)
                    else:
                        p.wait("scalar", sT)
                        p.run("scalar", lambda e, oc=oc, g=g, bank=bank, th=th:
                              e.activation(
                            th[:, oc * T + g * QG:oc * T + g * QG + QG],
                            bank[:, 0:512], AF.Tanh,
                            bias=bias_tile[:, oc:oc + 1]), inc=sS)
            p.wait("tensor", esem)
        p.wait("vector", sS)
        for j in range(NC):
            p.run("vector", lambda e, j=j: e.tensor_tensor(
                x_sb[:, ts(j, T)], x_sb[:, ts(j, T)], th[:, ts(j, T)], ALU.add),
                inc=sV if j == NC - 1 else None)

    def layer1():
        layernorm(do_tok=True)
        for u in range(NT):
            p.run("vector", lambda e, u=u: e.tensor_reduce(
                stat[:, 8 + u:9 + u], xn_tok[:, ts(u, D)], AX, ALU.max,
                apply_absolute_value=True))
        p.run("vector", lambda e: e.tensor_scalar(
            stat[:, 8:24], stat[:, 8:24], 1e-5, None, ALU.max), inc=sV)
        AO = xn_tok
        q2 = ab(arena, 0, BF16, T)
        k2 = ab(arena, 4096, BF16, T)
        v2 = ab(arena, 8192, BF16, T)
        Kse = ab(arena, 12288, BF16, 1024)
        Vse = ab(arena, 14336, BF16, 1024)
        Ktl = ab(arena, 16384, BF16, T)
        Vtl = ab(arena, 20480, BF16, T)
        lvl = ab(arena, 24576, BF16, 1024)
        ttmp = ab(arena, 26624, BF16, T)
        kseh = ab(arena, 30720, BF16, 1024)
        qh = ab(arena, 32768, BF16, T)
        M1 = ab(arena2, 33056, BF16, 512)
        cbn = ab(arena2, 34080, F16, 512)
        tv64 = ab(arena2, 35104, BF16, 512)
        Ssc = ab(arena2, 8224, F16, 8 * 512)
        Pbf = ab(arena2, 16416, BF16, 8 * 512)
        wq2 = ab(arena2, 24608, BF16, NC * 128)
        wk2 = ab(arena2, 26656, BF16, NC * 128)
        wv2 = ab(arena2, 28704, BF16, NC * 128)
        Vtok = ab(arena2, 30752, BF16, 2 * 8 * 65)
        IOT = ab(arena2, 0, F32, T)
        rcol = ab(arena2, 33024, F32, 8)
        RMIN = [3, 515, 1027, 1539, 7, 1031, 15, 31]
        p.wait("gpsimd", sT)
        p.wait("gpsimd", sV)
        p.wait("gpsimd", sS)
        p.dma("gpsimd", IOT[:, 0:T], dram["IOTA_T"], sG)
        p.dma("gpsimd", rcol[:, 0:8], dram["L1_ridx"], sG)
        p.run("gpsimd", lambda e: e.memset(kseh[64:96, 0:1024], 0.0))
        p.run("gpsimd", lambda e: e.memset(kseh[96:128, 0:1024], 0.0))
        p.run("gpsimd", lambda e: e.memset(kseh[64:65, 0:1024], 64.0))
        p.run("gpsimd", lambda e: e.memset(qh[64:96, 0:T], 0.0))
        p.run("gpsimd", lambda e: e.memset(qh[96:128, 0:T], 0.0))
        p.run("gpsimd", lambda e: e.memset(qh[64:65, 0:T], 1.0), inc=sE)
        gneg = stat[:, 24:25]
        tap("xn", xn_bf[:, 0:NC * T], [128, NC * T], BF16)
        for hp in range(8 if L1_ATTN else 0):
            if hp > 0:
                p.wait("gpsimd", sT)
            chunked_load("gpsimd", wq2[:, 0:NC * 128],
                         dram["L1_WattnT"][:, 128 * hp:128 * hp + 128], 128, sG)
            chunked_load("gpsimd", wk2[:, 0:NC * 128],
                         dram["L1_WattnT"][:, 1024 + 128 * hp:1152 + 128 * hp],
                         128, sG)
            chunked_load("gpsimd", wv2[:, 0:NC * 128],
                         dram["L1_WattnT"][:, 2048 + 128 * hp:2176 + 128 * hp],
                         128, sG)
            p.run("gpsimd", lambda e: e.memset(Kse[:, 1023:1024], 0.0))
            p.run("gpsimd", lambda e: e.memset(Vse[:, 1023:1024], 0.0))
            p.run("gpsimd", lambda e: e.memset(
                Vtok[:].rearrange("p (a b) -> p a b", b=65)[:, :, 64:65], 1.0),
                inc=sE)
            p.wait("tensor", sG)
            p.wait("tensor", sE)
            p.wait("tensor", sV)
            ev0 = sV.v
            it = 0
            for dst, wsl_ in ((q2, wq2), (k2, wk2), (v2, wv2)):
                for g in range(NQG):
                    bank = pb[it % 2]
                    if it >= 2:
                        p.wait("tensor", sV, ev0 + it - 1)
                    for j in range(NC):
                        p.run("tensor", lambda e, j=j, g=g, bank=bank,
                              wsl_=wsl_: e.matmul(
                            bank[:, 0:512], wsl_[:, ts(j, 128)],
                            xn_bf[:, j * T + g * QG:j * T + g * QG + QG],
                            start=(j == 0), stop=(j == NC - 1)),
                            inc=sT if j == NC - 1 else None)
                    p.wait("vector", sT)
                    if dst is q2:
                        p.run("vector", lambda e, dst=dst, g=g, bank=bank:
                              e.tensor_scalar(dst[:, ts(g, QG)], bank[:, 0:512],
                                              0.125, None, ALU.mult), inc=sV)
                    else:
                        p.run("vector", lambda e, dst=dst, g=g, bank=bank:
                              e.tensor_copy(dst[:, ts(g, QG)], bank[:, 0:512]),
                              inc=sV)
                    it += 1
            def seg(dst, src):
                s3 = src[:].rearrange("p (a b) -> p a b", b=2)
                d2 = lvl[:, 0:1024].unsqueeze(2)
                p.run("vector", lambda e, s3=s3, d2=d2: e.tensor_tensor(
                    d2, s3[:, :, 0:1], s3[:, :, 1:2], ALU.add), inc=sV)
                p.wait("vector", sV)
                off = 0
                w = 512
                srcap = lvl[:, 0:1024]
                while w >= 1:
                    sv = srcap.rearrange("p (a b) -> p a b", b=2)
                    p.run("vector", lambda e, sv=sv, dst=dst, off=off, w=w:
                          e.tensor_tensor(dst[:, off:off + w].unsqueeze(2),
                                          sv[:, :, 0:1], sv[:, :, 1:2],
                                          ALU.add), inc=sV)
                    p.wait("vector", sV)
                    srcap = dst[:, off:off + w]
                    off += w
                    w //= 2
            seg(Kse, k2)
            seg(Vse, v2)
            def tail(dst, src):
                s4 = src[:].rearrange("p (a b) -> p a b", b=4)
                d4 = dst[:].rearrange("p (a b) -> p a b", b=4)
                p.run("vector", lambda e, s4=s4, d4=d4: e.tensor_copy(
                    d4[:, :, 0:1].squeeze(2), s4[:, :, 0:1].squeeze(2)),
                    inc=sV)
                p.wait("vector", sV)
                for r in range(1, 4):
                    p.run("vector", lambda e, s4=s4, d4=d4, r=r: e.tensor_tensor(
                        d4[:, :, r:r + 1].squeeze(2),
                        d4[:, :, r - 1:r].squeeze(2),
                        s4[:, :, r:r + 1].squeeze(2), ALU.add), inc=sV)
                    p.wait("vector", sV)
            tail(Ktl, k2)
            tail(Vtl, v2)
            if hp == 0:
                tap("q2", q2[:, 0:T], [128, T], BF16)
                tap("k2", k2[:, 0:T], [128, T], BF16)
                tap("v2", v2[:, 0:T], [128, T], BF16)
                tap("Kse", Kse[:, 0:1024], [128, 1024], BF16)
                tap("Vse", Vse[:, 0:1024], [128, 1024], BF16)
                tap("Ktl", Ktl[:, 0:T], [128, T], BF16)
                tap("Vtl", Vtl[:, 0:T], [128, T], BF16)
            p.run("vector", lambda e: e.tensor_tensor(
                ttmp[:, 0:T], q2[:, 0:T], Ktl[:, 0:T], ALU.mult), inc=sV)
            p.wait("tensor", sV)
            ev0 = sV.v
            for i2 in range(16):
                h01, sc = divmod(i2, 8)
                bank = pb[i2 % 2]
                bb = bank[:].bitcast(BF16)
                if i2 >= 2:
                    p.wait("tensor", sV, ev0 + i2 - 1)
                p.run("tensor", lambda e, h01=h01, sc=sc, bb=bb: e.transpose(
                    bb[:, 0:64], Vse[64 * h01:64 * h01 + 64, ts(sc, 128)],
                    identb[64 * h01:64 * h01 + 64, 64 * h01:64 * h01 + 64]), inc=sT)
                p.wait("vector", sT)
                p.run("vector", lambda e, h01=h01, sc=sc, bb=bb: e.tensor_copy(
                    Vtok[:, 520 * h01 + 65 * sc:520 * h01 + 65 * sc + 64],
                    bb[:, 0:64]), inc=sV)
            p.wait("tensor", sV)
            for h01 in range(2):
                hb = 64 * h01
                p.run("vector", lambda e: e.nop(), inc=sV)
                p.wait("gpsimd", sV)
                p.wait("gpsimd", sT)
                p.dma("gpsimd", kseh[0:64, 0:1024], Kse[hb:hb + 64, 0:1024], sG)
                p.dma("gpsimd", qh[0:64, 0:T], q2[hb:hb + 64, 0:T], sG)
                p.wait("tensor", sG)
                p.wait("vector", sG)
                for g in range(NQG):
                    sv0 = sV.v
                    nsc = 0
                    for sc in range(8):
                        if RMIN[sc] > g * QG + QG - 1:
                            p.run("vector", lambda e, sc=sc: e.memset(
                                Ssc[:, ts(sc, 512)], 0.0))
                            continue
                        bank = pb[nsc % 2]
                        if nsc >= 2:
                            p.wait("tensor", sV, sv0 + (nsc - 2) + 1)
                        p.run("tensor", lambda e, sc=sc, g=g, bank=bank:
                              e.matmul(
                            bank[:, 0:512], kseh[0:65, ts(sc, 128)],
                            qh[0:65, ts(g, QG)], start=True, stop=True),
                            inc=sT)
                        p.wait("vector", sT)
                        p.run("vector", lambda e, sc=sc, g=g, bank=bank:
                              e.scalar_tensor_tensor(
                            Ssc[:, ts(sc, 512)], IOT[:, ts(g, QG)],
                            rcol[:, sc:sc + 1], bank[:, 0:512],
                            ALU.is_ge, ALU.mult), inc=sV)
                        nsc += 1
                    tb = [ab(arena2, 16416 + i * 1024, BF16, 512)
                          for i in range(4)]
                    for i in range(4):
                        p.run("vector", lambda e, i=i: e.tensor_tensor(
                            tb[i][:, 0:512], Ssc[:, ts(2 * i, 512)],
                            Ssc[:, ts(2 * i + 1, 512)], ALU.max))
                    p.run("vector", lambda e: e.drain())
                    p.run("vector", lambda e: e.tensor_tensor(
                        tb[0][:, 0:512], tb[0][:, 0:512], tb[1][:, 0:512],
                        ALU.max))
                    p.run("vector", lambda e: e.tensor_tensor(
                        tb[2][:, 0:512], tb[2][:, 0:512], tb[3][:, 0:512],
                        ALU.max))
                    p.run("vector", lambda e: e.drain())
                    p.run("vector", lambda e: e.tensor_tensor(
                        M1[:, 0:512], tb[0][:, 0:512], tb[2][:, 0:512],
                        ALU.max))
                    p.run("vector", lambda e: e.drain())
                    p.run("vector", lambda e: e.nop(), inc=sV)
                    p.wait("tensor", sV)
                    bb16 = pb[4][:].bitcast(BF16)
                    for i in range(4):
                        p.run("tensor", lambda e, i=i, bb16=bb16: e.transpose(
                            bb16[:, ts(i, 128)], M1[:, ts(i, 128)], identb[:]),
                            inc=sT if i == 3 else None)
                    p.wait("vector", sT)
                    for i in range(4):
                        p.run("vector", lambda e, i=i, bb16=bb16: e.tensor_reduce(
                            stat[:, 25 + i:26 + i], bb16[:, ts(i, 128)], AX,
                            ALU.max))
                    p.run("vector", lambda e: e.tensor_scalar(
                        stat[:, 25:29], stat[:, 25:29], 64.0, None, ALU.max))
                    p.run("vector", lambda e: e.tensor_scalar(
                        stat[:, 25:29], stat[:, 25:29], -1.0, None, ALU.mult),
                        inc=sV)
                    p.wait("tensor", sV)
                    p.run("tensor", lambda e: e.transpose(
                        pb[3][0:4, 0:128], stat[:, 25:29], identf[:]), inc=sT)
                    p.wait("vector", sT)
                    stg4 = ab(arena2, 33056, F32, 128)
                    p.run("vector", lambda e, stg4=stg4: e.tensor_copy(
                        stg4[0:4, 0:128], pb[3][0:4, 0:128]), inc=sV)
                    p.wait("gpsimd", sV)
                    p.dma("gpsimd", rows[0:1, 0:512], stg4[0:4, 0:128], sG)
                    p.wait("vector", sG)
                    p.run("vector", lambda e: e.tensor_copy(
                        rows_bf[0:1, 0:512], rows[0:1, 0:512]), inc=sV)
                    p.wait("tensor", sV)
                    p.run("tensor", lambda e: e.matmul(
                        pb[4][:, 0:512], ones_row[:], rows_bf[0:1, 0:512],
                        start=True, stop=True), inc=sT)
                    p.wait("vector", sT)
                    p.run("vector", lambda e: e.tensor_copy(
                        cbn[:, 0:512], pb[4][:, 0:512]))
                    p.run("vector", lambda e: e.drain())
                    for sc in range(8):
                        p.run("vector", lambda e, sc=sc: e.tensor_tensor(
                            Ssc[:, ts(sc, 512)], Ssc[:, ts(sc, 512)],
                            cbn[:, 0:512], ALU.add))
                    p.run("vector", lambda e: e.drain())
                    p.run("vector", lambda e: e.nop(), inc=sV)
                    p.wait("scalar", sV)
                    p.run("scalar", lambda e: e.activation(
                        Pbf[:, 0:8 * 512], Ssc[:, 0:8 * 512], AF.Exp), inc=sS)
                    p.wait("tensor", sS)
                    for sc in range(8):
                        p.run("tensor", lambda e, sc=sc, h01=h01: e.matmul(
                            pb[2][0:65, 0:512],
                            Vtok[:, 520 * h01 + 65 * sc:520 * h01 + 65 * sc + 65],
                            Pbf[:, ts(sc, 512)],
                            start=(sc == 0), stop=(sc == 7)),
                            inc=sT if sc == 7 else None)
                    p.run("tensor", lambda e, hb=hb, g=g: e.matmul(
                        pb[5][0:1, 0:512], ones_col[hb:hb + 64, 0:1],
                        ttmp[hb:hb + 64, ts(g, QG)],
                        start=True, stop=True), inc=sT)
                    p.wait("vector", sT)
                    p.run("vector", lambda e: e.tensor_copy(
                        rows[0:1, 512:1024], pb[5][0:1, 0:512]))
                    p.run("vector", lambda e: e.tensor_tensor(
                        rows[0:1, 512:1024], rows[0:1, 512:1024],
                        rows[0:1, 0:512], ALU.add))
                    p.run("vector", lambda e: e.tensor_scalar(
                        rows[0:1, 512:1024], rows[0:1, 512:1024], 64.0,
                        None, ALU.add), inc=sV)
                    p.wait("scalar", sV)
                    p.run("scalar", lambda e: e.activation(
                        rows[0:1, 512:1024], rows[0:1, 512:1024], AF.Exp),
                        inc=sS)
                    p.wait("vector", sS)
                    p.run("vector", lambda e: e.tensor_copy(
                        rows_bf[0:1, 512:1024], rows[0:1, 512:1024]))
                    p.run("vector", lambda e: e.tensor_copy(
                        rows[0:1, 0:512], pb[2][64:65, 0:512]))
                    p.run("vector", lambda e: e.tensor_tensor(
                        rows[0:1, 0:512], rows[0:1, 0:512],
                        rows[0:1, 512:1024], ALU.add))
                    p.run("vector", lambda e: e.reciprocal(
                        rows[0:1, 0:512], rows[0:1, 0:512]))
                    p.run("vector", lambda e: e.tensor_copy(
                        rows_bf[0:1, 0:512], rows[0:1, 0:512]), inc=sV)
                    p.wait("tensor", sV)
                    p.run("tensor", lambda e: e.matmul(
                        pb[3][:, 0:512], ones_row[:],
                        rows_bf[0:1, 512:1024], start=True, stop=True))
                    p.run("tensor", lambda e: e.matmul(
                        pb[4][:, 0:512], ones_row[:],
                        rows_bf[0:1, 0:512], start=True, stop=True), inc=sT)
                    p.wait("vector", sT)
                    vtl0 = ab(arena, 24576, BF16, 512)
                    if h01 == 0:
                        vtl_ap = Vtl[0:64, ts(g, QG)]
                    else:
                        p.run("vector", lambda e, g=g, vtl0=vtl0: e.tensor_copy(
                            vtl0[0:64, 0:512], Vtl[64:128, ts(g, QG)]))
                        vtl_ap = vtl0[0:64, 0:512]
                    p.run("vector", lambda e, vtl_ap=vtl_ap: e.tensor_tensor(
                        tv64[0:64, 0:512], pb[3][0:64, 0:512], vtl_ap,
                        ALU.mult))
                    p.run("vector", lambda e: e.tensor_tensor(
                        tv64[0:64, 0:512], tv64[0:64, 0:512],
                        pb[2][0:64, 0:512], ALU.add))
                    if h01 == 0:
                        p.run("vector", lambda e, hp=hp, g=g: e.tensor_tensor(
                            AO[0:64, hp * T + g * QG:hp * T + g * QG + QG],
                            tv64[0:64, 0:512], pb[4][0:64, 0:512], ALU.mult),
                            inc=sV)
                    else:
                        p.run("vector", lambda e: e.tensor_tensor(
                            tv64[0:64, 0:512], tv64[0:64, 0:512],
                            pb[4][0:64, 0:512], ALU.mult))
                        p.run("vector", lambda e, hp=hp, g=g: e.tensor_copy(
                            AO[64:128, hp * T + g * QG:hp * T + g * QG + QG],
                            tv64[0:64, 0:512]), inc=sV)
                    p.wait("tensor", sV)
        wpsl = ab(arena2, 24608, BF16, NC * 128)
        for oc in range(NC):
            p.wait("gpsimd", sT)
            chunked_load("gpsimd", wpsl[:, 0:NC * 128],
                         dram["L1_WprojT"][:, ts(oc, 128)], 128, sG)
            p.wait("tensor", sG)
            for g in range(NQG):
                it = oc * NQG + g
                bank = pb[it % 2]
                for j in range(NC):
                    p.run("tensor", lambda e, j=j, g=g, bank=bank: e.matmul(
                        bank[:, 0:512], wpsl[:, ts(j, 128)],
                        AO[:, j * T + g * QG:j * T + g * QG + QG],
                        start=(j == 0), stop=(j == NC - 1)),
                        inc=sT if j == NC - 1 else None)
                p.wait("vector", sT)
                p.run("vector", lambda e, oc=oc, g=g, bank=bank: e.tensor_tensor(
                    x_sb[:, oc * T + g * QG:oc * T + g * QG + QG],
                    x_sb[:, oc * T + g * QG:oc * T + g * QG + QG],
                    bank[:, 0:512], ALU.add), inc=sV)
                p.wait("tensor", sV)
        if L1_FFN:
            layer1_ffn()

    def layer1_ffn():
        wbig = ab(arena2, 0, BF16, NC * 512)
        wcol = {}
        for wi, wname in enumerate(("L1_wgT", "L1_wuT", "L1_wdT")):
            nin, nout = (D, 4096) if wi < 2 else (4096, D)
            nch = nin // 128
            wwid = 4096 // nch
            nloads = nout // wwid
            p.wait("gpsimd", sV)
            p.wait("gpsimd", sT)
            for ld in range(nloads):
                chunked_load("gpsimd", wbig[:, 0:nch * wwid],
                             dram[wname][:, ts(ld, wwid)], wwid, sG)
                p.wait("vector", sG)
                p.run("vector", lambda e, nch=nch, wwid=wwid: e.tensor_reduce(
                    stat[:, 25:26],
                    wbig[:, 0:nch * wwid], AX, ALU.add,
                    apply_absolute_value=True))
                if ld == 0:
                    p.run("vector", lambda e: e.tensor_scalar(
                        stat[:, 26:27], stat[:, 25:26], 1.0, None, ALU.mult),
                        inc=sV)
                else:
                    p.run("vector", lambda e: e.tensor_tensor(
                        stat[:, 26:27], stat[:, 26:27], stat[:, 25:26],
                        ALU.add), inc=sV)
                p.wait("gpsimd", sV)
            p.run("vector", lambda e: e.tensor_copy(
                rows_bf[0:1, 1024:1025].bitcast(F32)[0:1, 0:0] if False else
                stat[:, 25:26].bitcast(BF16)[:, 0:1], stat[:, 26:27]), inc=sV)
            p.wait("tensor", sV)
            p.run("tensor", lambda e: e.matmul(
                pb[5][0:1, 0:1], stat[:, 25:26].bitcast(BF16)[:, 0:1],
                ones_col[:], start=True, stop=True), inc=sT)
            p.wait("vector", sT)
            p.run("vector", lambda e, nin=nin, nout=nout: e.tensor_scalar(
                stat[0:1, 28:29], pb[5][0:1, 0:1], 1.0 / (nin * nout), None,
                ALU.mult))
            p.run("vector", lambda e: e.tensor_scalar(
                stat[0:1, 28:29], stat[0:1, 28:29], consts[0:1, 3:4], None,
                ALU.max))
            p.run("vector", lambda e: e.reciprocal(
                stat[0:1, 29:30], stat[0:1, 28:29]))
            p.run("vector", lambda e: e.tensor_copy(
                rows_bf[0:1, 1024:1026], stat[0:1, 28:30]), inc=sV)
            p.wait("tensor", sV)
            p.run("tensor", lambda e: e.matmul(
                pb[5][:, 0:2], ones_row[:], rows_bf[0:1, 1024:1026],
                start=True, stop=True), inc=sT)
            p.wait("vector", sT)
            p.run("vector", lambda e: e.tensor_copy(stat[:, 30:32],
                                                    pb[5][:, 0:2]), inc=sV)
            wf = ab(arena2, 8192, F32, NC * 128) if wi < 2 else \
                ab(arena2, 8192, F32, 16 * 128)
            wiq = ab(arena2, 24576, I32, NC * 128) if wi < 2 else \
                ab(arena2, 24576, I32, 16 * 128)
            wqb = ab(arena2, 4096, BF16, NC * 128) if wi < 2 else \
                ab(arena2, 4096, BF16, 16 * 128)
            scr = dram[f"L1_wq{wi}"]
            ncolchunks = nout // 128
            sub = 1 if wi < 2 else 2
            for cc in range(ncolchunks):
                for su in range(sub):
                    rows_slice = slice(su * 2048, su * 2048 + 2048) if sub == 2 \
                        else slice(0, nin)
                    nch2 = (2048 if sub == 2 else nin) // 128
                    p.wait("gpsimd", sV)
                    chunked_load("gpsimd", wf[:, 0:nch2 * 128],
                                 dram[wname][rows_slice, ts(cc, 128)], 128, sG)
                    p.wait("vector", sG)
                    p.run("vector", lambda e, nch2=nch2: e.tensor_scalar(
                        wf[:, 0:nch2 * 128], wf[:, 0:nch2 * 128],
                        stat[:, 31:32], None, ALU.mult))
                    p.run("vector", lambda e, nch2=nch2: e.tensor_copy(
                        wiq[:, 0:nch2 * 128], wf[:, 0:nch2 * 128]), inc=sV)
                    p.wait("scalar", sV)
                    p.run("scalar", lambda e, nch2=nch2: e.copy(
                        wf[:, 0:nch2 * 128], wiq[:, 0:nch2 * 128]), inc=sS)
                    p.wait("vector", sS)
                    p.run("vector", lambda e, nch2=nch2: e.tensor_scalar(
                        wqb[:, 0:nch2 * 128], wf[:, 0:nch2 * 128],
                        stat[:, 30:31], None, ALU.mult), inc=sV)
                    p.wait("sync", sV)
                    p.dma("sync",
                          scr[rows_slice, ts(cc, 128)].rearrange(
                              "(c p) o -> p c o", p=128),
                          wqb[:, 0:nch2 * 128].rearrange(
                              "p (c o) -> p c o", c=nch2), sW)
        AO = xn_tok
        xq = AO
        t1 = ab(arena2, 8192, F32, QG)
        ti = ab(arena2, 12288, I32, QG)
        for g in range(NQG):
            p.wait("tensor", sV)
            p.run("tensor", lambda e, g=g: e.transpose(
                pb[5][0:4, 0:128], stat[:, 8 + 4 * g:12 + 4 * g], identf[:]),
                inc=sT)
            p.wait("vector", sT)
            tvp = ab(arena2, 16384, F32, 128)
            p.run("vector", lambda e, tvp=tvp: e.tensor_copy(
                tvp[0:4, 0:128], pb[5][0:4, 0:128]), inc=sV)
            p.wait("gpsimd", sV)
            p.dma("gpsimd", rows[0:1, 0:512], tvp[0:4, 0:128], sG)
            p.wait("vector", sG)
            p.run("vector", lambda e: e.reciprocal(rows[0:1, 512:1024],
                                                   rows[0:1, 0:512]))
            p.run("vector", lambda e: e.tensor_scalar(
                rows[0:1, 512:1024], rows[0:1, 512:1024], 127.0, None,
                ALU.mult))
            p.run("vector", lambda e: e.tensor_scalar(
                rows[0:1, 0:512], rows[0:1, 0:512], 1.0 / 127.0, None,
                ALU.mult))
            p.run("vector", lambda e: e.tensor_copy(
                rows_bf[0:1, 0:512], rows[0:1, 512:1024]))
            p.run("vector", lambda e: e.tensor_copy(
                rows_bf[0:1, 512:1024], rows[0:1, 0:512]), inc=sV)
            p.wait("tensor", sV)
            p.run("tensor", lambda e: e.matmul(
                pb[4][:, 0:512], ones_row[:], rows_bf[0:1, 0:512],
                start=True, stop=True))
            p.run("tensor", lambda e: e.matmul(
                pb[5][:, 0:512], ones_row[:], rows_bf[0:1, 512:1024],
                start=True, stop=True), inc=sT)
            p.wait("vector", sT)
            for j in range(NC):
                p.run("vector", lambda e, j=j, g=g: e.tensor_tensor(
                    t1[:, 0:512], xn_bf[:, j * T + g * QG:j * T + g * QG + QG],
                    pb[4][:, 0:512], ALU.mult))
                p.run("vector", lambda e: e.tensor_copy(ti[:, 0:512],
                                                        t1[:, 0:512]), inc=sV)
                p.wait("scalar", sV)
                p.run("scalar", lambda e: e.copy(t1[:, 0:512], ti[:, 0:512]),
                      inc=sS)
                p.wait("vector", sS)
                p.run("vector", lambda e, j=j, g=g: e.tensor_tensor(
                    xq[:, j * T + g * QG:j * T + g * QG + QG],
                    t1[:, 0:512], pb[5][:, 0:512], ALU.mult),
                    inc=sV)
            p.wait("tensor", sV)
        TGW = 256
        gate_tg = ab(arena, 0, BF16, 32 * TGW)
        up_tg = ab(arena, 16384, BF16, 32 * TGW)
        wqg = ab(arena2, 0, BF16, NC * 128)
        s2rows = rows
        for tg in range(T // TGW):
            t0 = tg * TGW
            for stg in range(2):
                scr = dram[f"L1_wq{stg}"]
                esem = sS if stg == 0 else sV
                ev0 = esem.v
                for oc in range(32):
                    p.wait("sync", sT)
                    p.dma("sync", wqg[:, 0:NC * 128].rearrange(
                        "p (c o) -> p c o", c=NC),
                        scr[:, ts(oc, 128)].rearrange("(c p) o -> p c o", p=128),
                        sW)
                    p.wait("tensor", sW)
                    bank = pb[oc % 2]
                    if oc >= 2:
                        p.wait("tensor", esem, ev0 + oc - 1)
                    for j in range(NC):
                        p.run("tensor", lambda e, j=j, oc=oc, t0=t0, bank=bank:
                              e.matmul(
                            bank[:, 0:TGW], wqg[:, ts(j, 128)],
                            xq[:, j * T + t0:j * T + t0 + TGW],
                            start=(j == 0), stop=(j == NC - 1)),
                            inc=sT if j == NC - 1 else None)
                    if stg == 0:
                        p.wait("scalar", sT)
                        p.run("scalar", lambda e, oc=oc, bank=bank: e.activation(
                            gate_tg[:, ts(oc, TGW)], bank[:, 0:TGW], AF.Silu),
                            inc=sS)
                    else:
                        p.wait("vector", sT)
                        p.run("vector", lambda e, oc=oc, bank=bank: e.tensor_copy(
                            up_tg[:, ts(oc, TGW)], bank[:, 0:TGW]), inc=sV)
                p.wait("tensor", esem)
            p.wait("vector", sS)
            mxc = ab(arena2, 20480, F32, TGW)
            mnc = ab(arena2, 21504, F32, TGW)
            for oc in range(32):
                p.run("vector", lambda e, oc=oc: e.tensor_tensor(
                    gate_tg[:, ts(oc, TGW)], gate_tg[:, ts(oc, TGW)],
                    up_tg[:, ts(oc, TGW)], ALU.mult))
                if oc == 0:
                    p.run("vector", lambda e: e.tensor_scalar(
                        mxc[:, 0:TGW], gate_tg[:, 0:TGW], 1.0, None, ALU.mult))
                    p.run("vector", lambda e: e.tensor_scalar(
                        mnc[:, 0:TGW], gate_tg[:, 0:TGW], 1.0, None, ALU.mult))
                else:
                    p.run("vector", lambda e, oc=oc: e.tensor_tensor(
                        mxc[:, 0:TGW], mxc[:, 0:TGW], gate_tg[:, ts(oc, TGW)],
                        ALU.max))
                    p.run("vector", lambda e, oc=oc: e.tensor_tensor(
                        mnc[:, 0:TGW], mnc[:, 0:TGW], gate_tg[:, ts(oc, TGW)],
                        ALU.min))
            p.run("vector", lambda e: e.tensor_scalar(
                mnc[:, 0:TGW], mnc[:, 0:TGW], -1.0, None, ALU.mult))
            p.run("vector", lambda e: e.tensor_tensor(
                mxc[:, 0:TGW], mxc[:, 0:TGW], mnc[:, 0:TGW], ALU.max), inc=sV)
            p.wait("tensor", sV)
            for half in range(2):
                p.run("tensor", lambda e, half=half: e.transpose(
                    pb[5][:, ts(half, 128)], mxc[:, ts(half, 128)], identf[:]),
                    inc=sT if half == 1 else None)
            p.wait("vector", sT)
            p.run("vector", lambda e: e.tensor_reduce(
                stat[:, 25:27], pb[5][:, 0:256].rearrange(
                    "p (a b) -> p a b", b=128), AX, ALU.max))
            p.run("vector", lambda e: e.tensor_scalar(
                stat[:, 25:27], stat[:, 25:27], 1e-5, None, ALU.max), inc=sV)
            p.wait("tensor", sV)
            p.run("tensor", lambda e: e.transpose(
                pb[4][0:2, 0:128], stat[:, 25:27], identf[:]), inc=sT)
            p.wait("vector", sT)
            tvq = ab(arena2, 22528, F32, 128)
            p.run("vector", lambda e, tvq=tvq: e.tensor_copy(
                tvq[0:2, 0:128], pb[4][0:2, 0:128]), inc=sV)
            p.wait("gpsimd", sV)
            p.dma("gpsimd", rows[0:1, 0:TGW], tvq[0:2, 0:128], sG)
            p.wait("vector", sG)
            p.run("vector", lambda e: e.reciprocal(rows[0:1, 512:512 + TGW],
                                                   rows[0:1, 0:TGW]))
            p.run("vector", lambda e: e.tensor_scalar(
                rows[0:1, 512:512 + TGW], rows[0:1, 512:512 + TGW], 127.0,
                None, ALU.mult))
            p.run("vector", lambda e: e.tensor_scalar(
                rows[0:1, 0:TGW], rows[0:1, 0:TGW], 1.0 / 127.0, None,
                ALU.mult))
            p.run("vector", lambda e: e.tensor_copy(
                rows_bf[0:1, 0:TGW], rows[0:1, 512:512 + TGW]))
            p.run("vector", lambda e: e.tensor_copy(
                rows_bf[0:1, 512:512 + TGW], rows[0:1, 0:TGW]), inc=sV)
            p.wait("tensor", sV)
            p.run("tensor", lambda e: e.matmul(
                pb[4][:, 0:TGW], ones_row[:], rows_bf[0:1, 0:TGW],
                start=True, stop=True))
            p.run("tensor", lambda e: e.matmul(
                pb[5][:, 0:TGW], ones_row[:], rows_bf[0:1, 512:512 + TGW],
                start=True, stop=True), inc=sT)
            p.wait("vector", sT)
            for oc in range(32):
                p.run("vector", lambda e, oc=oc: e.tensor_tensor(
                    t1[:, 0:TGW], gate_tg[:, ts(oc, TGW)], pb[4][:, 0:TGW],
                    ALU.mult))
                p.run("vector", lambda e: e.tensor_copy(ti[:, 0:TGW],
                                                        t1[:, 0:TGW]), inc=sV)
                p.wait("scalar", sV)
                p.run("scalar", lambda e, oc=oc: e.copy(
                    gate_tg[:, ts(oc, TGW)], ti[:, 0:TGW]), inc=sS)
            p.wait("tensor", sS)
            rb2 = ab(arena2, 23552, F32, TGW)
            p.run("vector", lambda e, rb2=rb2: e.tensor_copy(
                rb2[:, 0:TGW], pb[5][:, 0:TGW]), inc=sV)
            wdq = ab(arena2, 24576, BF16, 32 * 128)
            for oo in range(NC):
                p.wait("sync", sT)
                p.dma("sync", wdq[:, 0:32 * 128].rearrange(
                    "p (c o) -> p c o", c=32),
                    dram["L1_wq2"][:, ts(oo, 128)].rearrange(
                        "(c p) o -> p c o", p=128), sW)
                p.wait("tensor", sW)
                bank = pb[oo % 2]
                for j in range(32):
                    p.run("tensor", lambda e, j=j, oo=oo, bank=bank: e.matmul(
                        bank[:, 0:TGW], wdq[:, ts(j, 128)],
                        gate_tg[:, ts(j, TGW)],
                        start=(j == 0), stop=(j == 31)),
                        inc=sT if j == 31 else None)
                p.wait("vector", sT)
                p.run("vector", lambda e, bank=bank, rb2=rb2: e.tensor_tensor(
                    t1[:, 0:TGW], bank[:, 0:TGW], rb2[:, 0:TGW], ALU.mult))
                p.run("vector", lambda e, oo=oo, t0=t0: e.tensor_tensor(
                    x_sb[:, oo * T + t0:oo * T + t0 + TGW],
                    x_sb[:, oo * T + t0:oo * T + t0 + TGW],
                    t1[:, 0:TGW], ALU.add), inc=sV)
                p.wait("tensor", sV)


    def layer2():
        layernorm(do_tok=False)
        eup_t = ab(arena, 32768, BF16, T)
        edn_t = ab(arena2, 8192, BF16, T)
        corr = ab(arena2, 12288, BF16, 4 * 512)
        iosc = ab(arena2, 16384, F32, T)
        p.wait("gpsimd", sV)
        p.wait("gpsimd", sS)
        p.wait("gpsimd", sT)
        p.run("gpsimd", lambda e: e.iota(iosc[:, 0:T], pattern=[[1, T]], base=0,
                                         channel_multiplier=0,
                                         allow_small_or_imprecise_dtypes=True))
        p.run("gpsimd", lambda e: e.nop(), inc=sE)
        p.wait("scalar", sE)
        p.run("scalar", lambda e: e.activation(eup_t[:, 0:T], iosc[:, 0:T],
                                               AF.Exp, scale=consts[:, 7:8]))
        p.run("scalar", lambda e: e.activation(edn_t[:, 0:T], iosc[:, 0:T],
                                               AF.Exp, scale=consts[:, 8:9]),
              inc=sS)
        cio = ab(arena2, 16384, F32, 512)
        for u in range(4):
            p.wait("gpsimd", sS)
            p.run("gpsimd", lambda e, u=u: e.iota(
                cio[:, 0:512], pattern=[[-1, 512]], base=128 * u,
                channel_multiplier=1, allow_small_or_imprecise_dtypes=True),
                inc=sE)
            p.wait("vector", sE)
            p.run("vector", lambda e: e.tensor_scalar(
                cio[:, 0:512], cio[:, 0:512], 0.0, None, ALU.max))
            p.run("vector", lambda e: e.tensor_scalar(
                cio[:, 0:512], cio[:, 0:512], -2e-4, None, ALU.mult), inc=sV)
            p.wait("scalar", sV)
            p.run("scalar", lambda e, u=u: e.activation(
                corr[:, ts(u, 512)], cio[:, 0:512], AF.Exp), inc=sS)
        kv_bf = ab(arena, 0, BF16, NC * T)
        wkb = ab(arena2, 16384, BF16, NC * 128)
        wvb = ab(arena2, 20480, BF16, NC * 128)
        p.wait("scalar", sV)
        ev0 = sV.v
        it = 0
        for oc in range(NC):
            p.wait("gpsimd", sT)
            p.wait("gpsimd", sS)
            chunked_load("gpsimd", wkb[:, 0:NC * 128],
                         dram["L2_Wk"][:, ts(oc, 128)], 128, sG)
            chunked_load("gpsimd", wvb[:, 0:NC * 128],
                         dram["L2_Wv"][:, ts(oc, 128)], 128, sG)
            p.wait("tensor", sG)
            for g in range(NQG):
                bank = pb[it % 2]
                if it >= 2:
                    p.wait("tensor", sV, ev0 + it - 1)
                for j in range(16):
                    wb = wkb if j < 8 else wvb
                    jj = j % 8
                    p.run("tensor", lambda e, j=j, jj=jj, g=g, bank=bank, wb=wb:
                          e.matmul(
                        bank[:, 0:512], wb[:, ts(jj, 128)],
                        xn_bf[:, jj * T + g * QG:jj * T + g * QG + QG],
                        start=(j == 0), stop=(j == 15)),
                        inc=sT if j == 15 else None)
                p.wait("vector", sT)
                p.run("vector", lambda e, oc=oc, g=g, bank=bank: e.tensor_copy(
                    kv_bf[:, oc * T + g * QG:oc * T + g * QG + QG],
                    bank[:, 0:512]), inc=sV)
                it += 1
        p.wait("tensor", sV)
        lat = ab(arena2, 0, BF16, 2 * T)
        wdkv = ab(arena2, 24576, BF16, NC * 256)
        p.wait("gpsimd", sT)
        chunked_load("gpsimd", wdkv[:, 0:NC * 256], dram["L2_Wdkv"], 256, sG)
        p.wait("tensor", sG)
        ev0 = sV.v
        it = 0
        for c in range(2):
            for g in range(NQG):
                bank = pb[it % 2]
                if it >= 2:
                    p.wait("tensor", sV, ev0 + it - 1)
                for j in range(NC):
                    p.run("tensor", lambda e, c=c, g=g, j=j, bank=bank: e.matmul(
                        bank[:, 0:512], wdkv[:, j * 256 + 128 * c:j * 256 + 128 * c + 128],
                        kv_bf[:, j * T + g * QG:j * T + g * QG + QG],
                        start=(j == 0), stop=(j == NC - 1)),
                        inc=sT if j == NC - 1 else None)
                p.wait("vector", sT)
                p.run("vector", lambda e, c=c, g=g, bank=bank: e.tensor_copy(
                    lat[:, c * T + g * QG:c * T + g * QG + QG], bank[:, 0:512]),
                    inc=sV)
                it += 1
        p.wait("tensor", sV)
        AO = xn_tok
        kup = ab(arena, 0, BF16, T)
        kdn = ab(arena, 4096, BF16, T)
        vrec = ab(arena, 8192, BF16, T)
        qdnh = ab(arena, 12288, BF16, T)
        quph = ab(arena, 16384, BF16, T)
        Vtok = ab(arena, 20480, BF16, 16 * 65)
        Pp = [ab(arena, 22560, BF16, 512), ab(arena, 23584, BF16, 512)]
        Sb = ab(arena, 24608, BF16, 512)
        tv64 = ab(arena, 25632, BF16, 512)
        rb = ab(arena2, 16384, F32, 512)
        wqh = ab(arena2, 18432, BF16, 8 * 64)
        wupkh = ab(arena2, 20480, BF16, 2 * 64)
        wupvh = ab(arena2, 21504, BF16, 2 * 64)
        p.wait("gpsimd", sT)
        p.run("gpsimd", lambda e: e.memset(
            Vtok[:].rearrange("p (a b) -> p a b", b=65)[:, :, 64:65], 1.0),
            inc=sE)
        p.wait("tensor", sE)
        for h in range(16 if L2_ATTN else 0):
            hb = 64 * (h % 2)
            p.wait("gpsimd", sT)
            p.wait("gpsimd", sV)
            chunked_load("gpsimd", wqh[:, 0:8 * 64],
                         dram["L2_Wq"][:, 64 * h:64 * h + 64], 64, sG)
            chunked_load("gpsimd", wupkh[:, 0:2 * 64],
                         dram["L2_Wupk"][:, 64 * h:64 * h + 64], 64, sG)
            chunked_load("gpsimd", wupvh[:, 0:2 * 64],
                         dram["L2_Wupv"][:, 64 * h:64 * h + 64], 64, sG)
            p.wait("tensor", sG)
            ev0 = sV.v
            it = 0
            for g in range(NQG):
                bank = pb[4 + it % 2]
                if it >= 2:
                    p.wait("tensor", sV, ev0 + it - 1)
                for j in range(NC):
                    p.run("tensor", lambda e, j=j, g=g, bank=bank: e.matmul(
                        bank[0:64, 0:512], wqh[:, ts(j, 64)],
                        xn_bf[:, j * T + g * QG:j * T + g * QG + QG],
                        start=(j == 0), stop=(j == NC - 1)),
                        inc=sT if j == NC - 1 else None)
                p.wait("vector", sT)
                p.run("vector", lambda e, g=g, bank=bank: e.scalar_tensor_tensor(
                    qdnh[0:64, ts(g, QG)], bank[0:64, 0:512], consts[0:64, 5:6],
                    edn_t[0:64, ts(g, QG)], ALU.mult, ALU.mult))
                p.run("vector", lambda e, g=g, bank=bank: e.scalar_tensor_tensor(
                    quph[0:64, ts(g, QG)], bank[0:64, 0:512], consts[0:64, 5:6],
                    eup_t[0:64, ts(g, QG)], ALU.mult, ALU.mult), inc=sV)
                it += 1
            for g in range(NQG):
                bank = pb[4 + it % 2]
                p.wait("tensor", sV, ev0 + it - 1)
                for c in range(2):
                    p.run("tensor", lambda e, c=c, g=g, bank=bank: e.matmul(
                        bank[0:64, 0:512], wupkh[:, ts(c, 64)],
                        lat[:, c * T + g * QG:c * T + g * QG + QG],
                        start=(c == 0), stop=(c == 1)),
                        inc=sT if c == 1 else None)
                p.wait("vector", sT)
                p.run("vector", lambda e, g=g, bank=bank: e.tensor_tensor(
                    kup[0:64, ts(g, QG)], bank[0:64, 0:512],
                    eup_t[0:64, ts(g, QG)], ALU.mult))
                p.run("vector", lambda e, g=g, bank=bank: e.tensor_tensor(
                    kdn[0:64, ts(g, QG)], bank[0:64, 0:512],
                    edn_t[0:64, ts(g, QG)], ALU.mult), inc=sV)
                it += 1
            for g in range(NQG):
                bank = pb[4 + it % 2]
                p.wait("tensor", sV, ev0 + it - 1)
                for c in range(2):
                    p.run("tensor", lambda e, c=c, g=g, bank=bank: e.matmul(
                        bank[0:64, 0:512], wupvh[:, ts(c, 64)],
                        lat[:, c * T + g * QG:c * T + g * QG + QG],
                        start=(c == 0), stop=(c == 1)),
                        inc=sT if c == 1 else None)
                p.wait("vector", sT)
                p.run("vector", lambda e, g=g, bank=bank: e.tensor_copy(
                    vrec[0:64, ts(g, QG)], bank[0:64, 0:512]), inc=sV)
                it += 1
            p.wait("tensor", sV)
            ev0 = sV.v
            for kk in range(NT):
                bank = pb[6 + kk % 2]
                bb = bank[:].bitcast(BF16)
                if kk >= 2:
                    p.wait("tensor", sV, ev0 + kk - 1)
                p.run("tensor", lambda e, kk=kk, bb=bb: e.transpose(
                    bb[:, 0:64], vrec[0:64, ts(kk, 128)], identb[0:64, 0:64]),
                    inc=sT)
                p.wait("vector", sT)
                p.run("vector", lambda e, kk=kk, bb=bb: e.tensor_copy(
                    Vtok[:, 65 * kk:65 * kk + 64], bb[:, 0:64]), inc=sV)
            p.wait("tensor", sV)
            for g in range(NQG):
                free_at = []
                av_done = []
                for kk in range(NT):
                    bank = pb[kk % 2]
                    pp = Pp[kk % 2]
                    if kk >= 2:
                        sem, val = free_at[kk - 2]
                        p.wait("tensor", sem, val)
                    straddle = 4 * g <= kk <= 4 * g + 3
                    lhsT = kdn if kk >= 4 * g + 4 else kup
                    rhs = quph if kk >= 4 * g + 4 else qdnh
                    p.run("tensor", lambda e, kk=kk, g=g, bank=bank, lhsT=lhsT,
                          rhs=rhs: e.matmul(
                        bank[:, 0:512], lhsT[0:64, ts(kk, 128)],
                        rhs[0:64, ts(g, QG)], start=True, stop=True), inc=sT)
                    if kk >= 2:
                        p.wait("scalar", sT, av_done[kk - 2])
                    if straddle:
                        u = kk - 4 * g
                        p.wait("vector", sT)
                        p.wait("vector", sS)
                        p.run("vector", lambda e, u=u, bank=bank: e.tensor_tensor(
                            Sb[:, 0:512], bank[:, 0:512], corr[:, ts(u, 512)],
                            ALU.mult), inc=sV)
                        free_at.append((sV, sV.v))
                        p.wait("scalar", sV)
                        p.run("scalar", lambda e, pp=pp: e.activation(
                            pp[:, 0:512], Sb[:, 0:512], AF.Exp), inc=sS)
                    else:
                        p.wait("scalar", sT)
                        p.run("scalar", lambda e, pp=pp, bank=bank: e.activation(
                            pp[:, 0:512], bank[:, 0:512], AF.Exp), inc=sS)
                        free_at.append((sS, sS.v))
                    p.wait("tensor", sS)
                    p.run("tensor", lambda e, kk=kk, pp=pp: e.matmul(
                        pb[2][0:65, 0:512], Vtok[:, 65 * kk:65 * kk + 65],
                        pp[:, 0:512], start=(kk == 0), stop=(kk == NT - 1)),
                        inc=sT)
                    av_done.append(sT.v)
                p.wait("vector", sT)
                p.run("vector", lambda e: e.tensor_copy(
                    rows[0:1, 0:512], pb[2][64:65, 0:512]))
                p.run("vector", lambda e: e.reciprocal(
                    rows[0:1, 0:512], rows[0:1, 0:512]))
                p.run("vector", lambda e: e.tensor_copy(
                    rows_bf[0:1, 0:512], rows[0:1, 0:512]), inc=sV)
                p.wait("tensor", sV)
                p.run("tensor", lambda e: e.matmul(
                    pb[3][:, 0:512], ones_row[:], rows_bf[0:1, 0:512],
                    start=True, stop=True), inc=sT)
                p.wait("vector", sT)
                p.run("vector", lambda e: e.tensor_copy(
                    rb[0:64, 0:512], pb[3][0:64, 0:512]))
                if hb == 0:
                    p.run("vector", lambda e, h=h, g=g: e.tensor_tensor(
                        AO[0:64, (h // 2) * T + g * QG:(h // 2) * T + g * QG + QG],
                        pb[2][0:64, 0:512], rb[0:64, 0:512], ALU.mult), inc=sV)
                else:
                    p.run("vector", lambda e: e.tensor_tensor(
                        tv64[0:64, 0:512], pb[2][0:64, 0:512], rb[0:64, 0:512],
                        ALU.mult))
                    p.run("vector", lambda e, h=h, g=g: e.tensor_copy(
                        AO[64:128, (h // 2) * T + g * QG:(h // 2) * T + g * QG + QG],
                        tv64[0:64, 0:512]), inc=sV)
                p.wait("tensor", sV)
        wpsl = ab(arena2, 24576, BF16, NC * 128)
        for oc in range(NC):
            p.wait("gpsimd", sT)
            chunked_load("gpsimd", wpsl[:, 0:NC * 128],
                         dram["L2_Wo"][:, ts(oc, 128)], 128, sG)
            p.wait("tensor", sG)
            for g in range(NQG):
                bank = pb[(oc * NQG + g) % 2]
                for j in range(NC):
                    p.run("tensor", lambda e, j=j, g=g, bank=bank: e.matmul(
                        bank[:, 0:512], wpsl[:, ts(j, 128)],
                        AO[:, j * T + g * QG:j * T + g * QG + QG],
                        start=(j == 0), stop=(j == NC - 1)),
                        inc=sT if j == NC - 1 else None)
                p.wait("vector", sT)
                p.run("vector", lambda e, oc=oc, g=g, bank=bank: e.tensor_tensor(
                    x_sb[:, oc * T + g * QG:oc * T + g * QG + QG],
                    x_sb[:, oc * T + g * QG:oc * T + g * QG + QG],
                    bank[:, 0:512], ALU.add), inc=sV)
                p.wait("tensor", sV)
        if L2_FFN:
            layer2_ffn()

    def qdq_strip(dst, src, rcol, scol, tf, ti, w):
        p.run("vector", lambda e: e.tensor_scalar(
            tf[:, 0:w], src, rcol, None, ALU.mult))
        p.run("vector", lambda e: e.tensor_copy(ti[:, 0:w], tf[:, 0:w]),
              inc=sV)
        p.wait("scalar", sV)
        p.run("scalar", lambda e: e.copy(tf[:, 0:w], ti[:, 0:w]), inc=sS)
        p.wait("vector", sS)
        p.run("vector", lambda e: e.tensor_scalar(
            dst, tf[:, 0:w], scol, None, ALU.mult), inc=sV)

    def row_allreduce_max(src_cols, n, out_cols):
        p.wait("tensor", sV)
        p.run("tensor", lambda e: e.transpose(
            pb[3][0:n, 0:128], stat[:, src_cols:src_cols + n], identf[:]),
            inc=sT)
        p.wait("vector", sT)
        tvp = ab(arena2, 35328, F32, 136)
        p.run("vector", lambda e: e.tensor_copy(
            tvp[0:n, 0:128], pb[3][0:n, 0:128]))
        p.run("vector", lambda e: e.drain())
        p.run("vector", lambda e: e.tensor_reduce(
            tvp[0:n, 128:129], tvp[0:n, 0:128], AX, ALU.max))
        p.run("vector", lambda e: e.drain())
        p.run("vector", lambda e: e.nop(), inc=sV)
        p.wait("gpsimd", sV)
        p.wait("gpsimd", sW)
        for i_ in range(n):
            p.dma("gpsimd", dram["cc_in"][0:1, i_:i_ + 1],
                  tvp[i_:i_ + 1, 128:129], sG)
        p.wait("gpsimd", sG)
        p.run("gpsimd", lambda e: e.collective_compute(
            "AllReduce", ALU.max, replica_groups=[CORES],
            ins=[cc_in_t.ap().opt()], outs=[cc_out_t.ap().opt()]),
            inc=sE)
        p.wait("gpsimd", sE)
        p.dma("gpsimd", stat[0:1, out_cols:out_cols + n],
              dram["cc_out"][0:1, 0:n], sG)
        p.wait("vector", sG)

    def bcast_cols(src_col, n, dst_col):
        p.run("vector", lambda e: e.tensor_copy(
            rows_bf[0:1, 1020:1020 + n], stat[0:1, src_col:src_col + n]),
            inc=sV)
        p.wait("tensor", sV)
        p.run("tensor", lambda e: e.matmul(
            pb[3][:, 0:n], ones_row[:], rows_bf[0:1, 1020:1020 + n],
            start=True, stop=True), inc=sT)
        p.wait("vector", sT)
        p.run("vector", lambda e: e.tensor_copy(
            stat[:, dst_col:dst_col + n], pb[3][:, 0:n]), inc=sV)

    def mk_sr(src_col, dst_col):
        p.run("vector", lambda e: e.tensor_scalar(
            stat[0:1, 23:24], stat[0:1, src_col:src_col + 1], 1.0 / 32767.0,
            None, ALU.mult))
        p.run("vector", lambda e: e.drain())
        p.run("vector", lambda e: e.tensor_scalar(
            stat[0:1, dst_col:dst_col + 1], stat[0:1, 23:24], 1e-12, None,
            ALU.max))
        p.run("vector", lambda e: e.drain())
        p.run("vector", lambda e: e.reciprocal(
            stat[0:1, dst_col + 1:dst_col + 2],
            stat[0:1, dst_col:dst_col + 1]))
        p.run("vector", lambda e: e.drain())
        p.run("vector", lambda e: e.nop(), inc=sV)

    def layer2_ffn():
        sTb = ab(arena, 0, BF16, 3072)
        rTb = ab(arena, 6144, BF16, 3072)
        wnat = ab(arena, 12288, F32, 3072)
        sbb = ab(arena2, 0, F32, 192)
        BDJ = ab(arena2, 768, BF16, 3072)
        wtr = ab(arena2, 6912, F32, 4096)
        wqs = ab(arena2, 23296, BF16, 4096)
        tf = ab(arena2, 31488, F32, 512)
        ti = ab(arena2, 33536, I32, 512)
        p.wait("gpsimd", sV)
        p.wait("gpsimd", sS)
        p.wait("gpsimd", sT)
        p.dma("gpsimd", BDJ[:, 0:3072], dram["BDJ"], sG)
        p.run("gpsimd", lambda e: e.memset(sTb[:, 0:3072], 1.0))
        p.run("gpsimd", lambda e: e.memset(rTb[:, 0:3072], 1.0), inc=sE)
        p.wait("tensor", sG)
        p.wait("tensor", sE)
        p.wait("vector", sE)
        specs = [("L2_wgN", "L2_wgT", "L2_wq0", 3072, 1024, 512),
                 ("L2_wuN", "L2_wuT", "L2_wq1", 3072, 1024, 512),
                 ("L2_wdN", "L2_wdT", "L2_wq2s", 1024, 3072, 128)]
        for nat, tr, scr, O, I, w in specs:
            nb = I // 32
            njc = I // 128
            for c in range(O // 128):
                p.wait("gpsimd", sV)
                p.dma("gpsimd", wnat[:, 0:I], dram[nat][ts(c, 128), :], sG)
                p.wait("vector", sG)
                p.run("vector", lambda e, I=I, nb=nb: e.tensor_reduce(
                    sbb[:, 0:nb], wnat[:, 0:I].rearrange(
                        "p (a b) -> p a b", b=32), AX, ALU.max,
                    apply_absolute_value=True), inc=sV)
                p.wait("vector", sV)
                p.run("vector", lambda e, nb=nb: e.tensor_scalar(
                    sbb[:, 0:nb], sbb[:, 0:nb], 1.0 / 7.0, None, ALU.mult))
                p.run("vector", lambda e, nb=nb: e.tensor_scalar(
                    sbb[:, 0:nb], sbb[:, 0:nb], 1e-12, None, ALU.max))
                p.run("vector", lambda e: e.drain())
                p.run("vector", lambda e, nb=nb: e.reciprocal(
                    sbb[:, 96:96 + nb], sbb[:, 0:nb]), inc=sV)
                p.wait("tensor", sV)
                p.run("tensor", lambda e, nb=nb: e.transpose(
                    pb[4][0:nb, 0:128], sbb[:, 0:nb], identf[:]))
                p.run("tensor", lambda e, nb=nb: e.transpose(
                    pb[5][0:nb, 0:128], sbb[:, 96:96 + nb], identf[:]), inc=sT)
                p.wait("vector", sT)
                p.run("vector", lambda e, c=c, nb=nb: e.tensor_copy(
                    sTb[0:nb, ts(c, 128)], pb[4][0:nb, 0:128]))
                p.run("vector", lambda e, c=c, nb=nb: e.tensor_copy(
                    rTb[0:nb, ts(c, 128)], pb[5][0:nb, 0:128]), inc=sV)
            for cs in range(O // w):
                p.wait("gpsimd", sV)
                p.wait("gpsimd", sW)
                chunked_load("gpsimd", wtr[:, 0:njc * w],
                             dram[tr][:, ts(cs, w)], w, sG)
                p.wait("tensor", sG)
                sv0 = sV.v
                for j in range(njc):
                    bR = pb[4 + j % 2]
                    bS = pb[6 + j % 2]
                    if j >= 2:
                        p.wait("tensor", sV, sv0 + 2 * (j - 1))
                    p.run("tensor", lambda e, j=j, cs=cs, w=w, bR=bR: e.matmul(
                        bR[:, 0:w], BDJ[:, ts(j, 128)],
                        rTb[0:128, cs * w:cs * w + w],
                        start=True, stop=True))
                    p.run("tensor", lambda e, j=j, cs=cs, w=w, bS=bS: e.matmul(
                        bS[:, 0:w], BDJ[:, ts(j, 128)],
                        sTb[0:128, cs * w:cs * w + w],
                        start=True, stop=True), inc=sT)
                    p.wait("vector", sT)
                    p.run("vector", lambda e, j=j, w=w, bR=bR: e.tensor_tensor(
                        tf[:, 0:w], wtr[:, j * w:j * w + w], bR[:, 0:w],
                        ALU.mult))
                    p.run("vector", lambda e, w=w: e.tensor_copy(
                        ti[:, 0:w], tf[:, 0:w]), inc=sV)
                    p.wait("scalar", sV)
                    p.run("scalar", lambda e, w=w: e.copy(
                        tf[:, 0:w], ti[:, 0:w]), inc=sS)
                    p.wait("vector", sS)
                    p.run("vector", lambda e, j=j, w=w, bS=bS: e.tensor_tensor(
                        wqs[:, j * w:j * w + w], tf[:, 0:w], bS[:, 0:w],
                        ALU.mult), inc=sV)
                p.wait("sync", sV)
                p.dma("sync",
                      dram[scr][:, ts(cs, w)].rearrange("(c p) o -> p c o", p=128),
                      wqs[:, 0:njc * w].rearrange("p (c o) -> p c o", c=njc),
                      sW)
        if L2FFN_PH < 2:
            return
        xq = xn_tok
        for j in range(NC):
            p.run("vector", lambda e, j=j: e.tensor_reduce(
                stat[:, 8 + j:9 + j], xn_bf[:, ts(j, T)], AX, ALU.max,
                apply_absolute_value=True))
        p.run("vector", lambda e: e.tensor_reduce(
            stat[:, 24:25], stat[:, 8:16], AX, ALU.max), inc=sV)
        row_allreduce_max(24, 1, 24)
        mk_sr(24, 26)
        bcast_cols(26, 2, 26)
        tf2 = ab(arena2, 0, F32, 1536)
        ti2 = ab(arena2, 8192, I32, 1536)
        for j in range(NC):
            for g in range(NQG):
                qdq_strip(xq[:, j * T + g * QG:j * T + g * QG + QG],
                          xn_bf[:, j * T + g * QG:j * T + g * QG + QG],
                          stat[:, 27:28], stat[:, 26:27], tf2, ti2, QG)
        if L2FFN_PH < 3:
            return
        wub = ab(arena, 0, BF16, 1024)
        wgb = ab(arena, 2048, BF16, 1024)
        gstage = ab(arena, 4096, BF16, 512)
        mbuf = ab(arena, 5120, F32, 192)
        scrs = ["L2_wq0", "L2_wq1"]
        raws = ["L2_gtraw", "L2_upraw"]
        p.wait("vector", sS)
        p.wait("vector", sT)
        p.wait("gpsimd", sW)
        for oc in range(L2A_N):
            p.wait("gpsimd", sT)
            chunked_load("gpsimd", wgb[:, 0:1024],
                         dram["L2_wq0"][:, ts(oc, 128)], 128, sG)
            chunked_load("gpsimd", wub[:, 0:1024],
                         dram["L2_wq1"][:, ts(oc, 128)], 128, sG)
            p.wait("tensor", sG)
            for wi in range(2):
                wb = [wgb, wub][wi]
                for g in range(NQG):
                    it = wi * NQG + g
                    bank = pb[it % 2]
                    if oc * 8 + it >= 2:
                        p.wait("tensor", sS)
                        p.wait("tensor", sV)
                    for j in range(NC):
                        p.run("tensor", lambda e, j=j, g=g, bank=bank, wb=wb:
                              e.matmul(
                            bank[:, 0:512], wb[:, ts(j, 128)],
                            xq[:, j * T + g * QG:j * T + g * QG + QG],
                            start=(j == 0), stop=(j == NC - 1)),
                            inc=sT if j == NC - 1 else None)
                    p.wait("scalar", sT)
                    p.wait("scalar", sW, sW.v)
                    p.wait("scalar", sV)
                    p.run("scalar", lambda e, bank=bank: e.activation(
                        gstage[:, 0:512], bank[:, 0:512], AF.Identity), inc=sS)
                    p.wait("vector", sS)
                    p.run("vector", lambda e, wi=wi, oc=oc, g=g:
                          e.tensor_reduce(
                        mbuf[:, wi * 96 + oc * 4 + g:wi * 96 + oc * 4 + g + 1],
                        gstage[:, 0:512], AX, ALU.max,
                        apply_absolute_value=True), inc=sV)
                    p.wait("sync", sS)
                    p.dma("sync", dram[raws[wi]][ts(oc, 128), ts(g, 512)],
                          gstage[:, 0:512], sW)
        if not L2A_COLL:
            return
        p.run("vector", lambda e: e.tensor_reduce(
            stat[:, 24:25], mbuf[:, 0:96], AX, ALU.max))
        p.run("vector", lambda e: e.tensor_reduce(
            stat[:, 25:26], mbuf[:, 96:192], AX, ALU.max), inc=sV)
        row_allreduce_max(24, 2, 24)
        mk_sr(24, 26)
        mk_sr(25, 28)
        bcast_cols(26, 4, 26)
        if L2FFN_PH < 4:
            return
        ubuf = ab(arena, 8192, BF16, 6144)
        gbuf = ab(arena, 20480, BF16, 6144)
        TGW = 256
        for tg in range(T // TGW):
            p.wait("gpsimd", sV)
            p.wait("gpsimd", sW)
            chunked_load("gpsimd", ubuf[:, 0:6144],
                         dram["L2_upraw"][:, ts(tg, TGW)], TGW, sG)
            chunked_load("gpsimd", gbuf[:, 0:6144],
                         dram["L2_gtraw"][:, ts(tg, TGW)], TGW, sG)
            p.wait("vector", sG)
            for s4 in range(4):
                sl = slice(s4 * 1536, s4 * 1536 + 1536)
                qdq_strip(gbuf[:, sl], gbuf[:, sl], stat[:, 27:28],
                          stat[:, 26:27], tf2, ti2, 1536)
                p.wait("scalar", sV)
                p.run("scalar", lambda e, sl=sl: e.activation(
                    tf2[:, 0:1536], gbuf[:, sl], AF.Sigmoid), inc=sS)
                p.wait("vector", sS)
                p.run("vector", lambda e: e.tensor_scalar(
                    tf2[:, 0:1536], tf2[:, 0:1536], 65536.0, None, ALU.mult))
                p.run("vector", lambda e: e.tensor_copy(
                    ti2[:, 0:1536], tf2[:, 0:1536]), inc=sV)
                p.wait("scalar", sV)
                p.run("scalar", lambda e: e.copy(
                    tf2[:, 0:1536], ti2[:, 0:1536]), inc=sS)
                p.wait("vector", sS)
                p.run("vector", lambda e: e.tensor_scalar(
                    tf2[:, 0:1536], tf2[:, 0:1536], 65535.0, None, ALU.min))
                p.run("vector", lambda e: e.tensor_scalar(
                    tf2[:, 0:1536], tf2[:, 0:1536], 1.0 / 65536.0, None,
                    ALU.mult))
                p.run("vector", lambda e, sl=sl: e.tensor_tensor(
                    gbuf[:, sl], gbuf[:, sl], tf2[:, 0:1536], ALU.mult),
                    inc=sV)
                p.wait("vector", sV)
                p.run("vector", lambda e, tg=tg, s4=s4, sl=sl: e.tensor_reduce(
                    mbuf[:, tg * 4 + s4:tg * 4 + s4 + 1], gbuf[:, sl], AX,
                    ALU.max, apply_absolute_value=True), inc=sV)
            p.wait("sync", sV)
            p.dma("sync", dram["L2_t3"][:, ts(tg, TGW)].rearrange(
                "(c p) o -> p c o", p=128),
                gbuf[:, 0:6144].rearrange("p (c o) -> p c o", c=24), sW)
        p.run("vector", lambda e: e.tensor_reduce(
            stat[:, 24:25], mbuf[:, 0:32], AX, ALU.max), inc=sV)
        row_allreduce_max(24, 1, 24)
        mk_sr(24, 30)
        bcast_cols(30, 2, 30)
        for tg in range(T // TGW):
            p.wait("gpsimd", sV)
            p.wait("gpsimd", sW)
            chunked_load("gpsimd", gbuf[:, 0:6144],
                         dram["L2_t3"][:, ts(tg, TGW)], TGW, sG)
            chunked_load("gpsimd", ubuf[:, 0:6144],
                         dram["L2_upraw"][:, ts(tg, TGW)], TGW, sG)
            p.wait("vector", sG)
            for s4 in range(4):
                sl = slice(s4 * 1536, s4 * 1536 + 1536)
                qdq_strip(gbuf[:, sl], gbuf[:, sl], stat[:, 31:32],
                          stat[:, 30:31], tf2, ti2, 1536)
                qdq_strip(ubuf[:, sl], ubuf[:, sl], stat[:, 29:30],
                          stat[:, 28:29], tf2, ti2, 1536)
                p.run("vector", lambda e, sl=sl: e.tensor_tensor(
                    gbuf[:, sl], gbuf[:, sl], ubuf[:, sl], ALU.mult), inc=sV)
                p.wait("vector", sV)
                p.run("vector", lambda e, tg=tg, s4=s4, sl=sl: e.tensor_reduce(
                    mbuf[:, tg * 4 + s4:tg * 4 + s4 + 1], gbuf[:, sl], AX,
                    ALU.max, apply_absolute_value=True), inc=sV)
            p.wait("sync", sV)
            p.dma("sync", dram["L2_gtraw"][:, ts(tg, TGW)].rearrange(
                "(c p) o -> p c o", p=128),
                gbuf[:, 0:6144].rearrange("p (c o) -> p c o", c=24), sW)
        p.run("vector", lambda e: e.tensor_reduce(
            stat[:, 24:25], mbuf[:, 0:32], AX, ALU.max), inc=sV)
        row_allreduce_max(24, 1, 24)
        mk_sr(24, 26)
        bcast_cols(26, 2, 26)
        if L2FFN_PH < 5:
            return
        wdq = ab(arena2, 16384, BF16, 3072)
        for tg in range(L2D_N):
            p.wait("gpsimd", sV)
            p.wait("gpsimd", sW)
            chunked_load("gpsimd", gbuf[:, 0:6144],
                         dram["L2_gtraw"][:, ts(tg, TGW)], TGW, sG)
            p.wait("vector", sG)
            for s4 in range(4):
                sl = slice(s4 * 1536, s4 * 1536 + 1536)
                qdq_strip(gbuf[:, sl], gbuf[:, sl], stat[:, 27:28],
                          stat[:, 26:27], tf2, ti2, 1536)
            for oo in range(NC if L2D_MM else 0):
                p.wait("sync", sT)
                p.dma("sync", wdq[:, 0:3072].rearrange("p (c o) -> p c o", c=24),
                      dram["L2_wq2s"][:, ts(oo, 128)].rearrange(
                          "(c p) o -> p c o", p=128), sW)
                p.wait("tensor", sW)
                p.wait("tensor", sV)
                bank = pb[oo % 2]
                for j in range(24):
                    p.run("tensor", lambda e, j=j, oo=oo, bank=bank: e.matmul(
                        bank[:, 0:TGW], wdq[:, ts(j, 128)],
                        gbuf[:, ts(j, TGW)],
                        start=(j == 0), stop=(j == 23)),
                        inc=sT if j == 23 else None)
                p.wait("vector", sT)
                p.run("vector", lambda e, oo=oo, tg=tg, bank=bank: e.tensor_tensor(
                    x_sb[:, oo * T + tg * TGW:oo * T + tg * TGW + TGW],
                    x_sb[:, oo * T + tg * TGW:oo * T + tg * TGW + TGW],
                    bank[:, 0:TGW], ALU.add), inc=sV)
                p.wait("tensor", sV)

    def layer3():
        NCO = len(CORES)
        layernorm(do_tok=True)
        KWb = ((NCO + 1) // 2) * T * 2
        k_sb = ab(arena, 0, BF16, KWb // 2)
        v_sb = ab(arena, KWb, BF16, KWb // 2)
        q2 = ab(arena, 2 * KWb, BF16, T)
        NKK = 16 * NCO
        Vtok = ab(arena2, 0, BF16, 65 * NKK)
        y2 = ab(arena2, 16640, F32, T)
        vloc = ab(arena2, 16640, BF16, T)
        Pp = [ab(arena2, 24832, BF16, 512), ab(arena2, 25856, BF16, 512)]
        rb = ab(arena2, 26880, F32, 512)
        kvstack = ab(arena2, 28928, BF16, T)
        wq3 = ab(arena2, 33024, BF16, 8 * 64)
        wk3 = ab(arena2, 34048, BF16, 8 * 64)
        wv3 = ab(arena2, 35072, BF16, 8 * 64)
        p.wait("gpsimd", sV)
        p.wait("gpsimd", sS)
        p.wait("gpsimd", sT)
        chunked_load("gpsimd", wq3[:, 0:512], dram["L3_WqT"], 64, sG)
        chunked_load("gpsimd", wk3[:, 0:512], dram["L3_WkT"], 64, sG)
        chunked_load("gpsimd", wv3[:, 0:512], dram["L3_WvT"], 64, sG)
        p.dma("gpsimd", stat[0:64, 20:23], dram["L3_bqkv"], sG)
        p.wait("vector", sG)
        p.wait("tensor", sG)
        p.run("vector", lambda e: e.tensor_scalar(
            stat[0:64, 23:24], stat[0:64, 20:21], 1.0 / 32.0, None, ALU.mult),
            inc=sV)
        p.wait("scalar", sV)
        if not L3_ATTN:
            layer3_ffn()
            return
        ev0 = sV.v
        es0 = sS.v
        it = 0
        for which in range(3):
            wsl = [wq3, wk3, wv3][which]
            for g in range(NQG):
                bank = pb[4 + it % 2]
                if it >= 2:
                    if it - 2 < 4:
                        p.wait("tensor", sS, es0 + (it - 2) + 1 - 0)
                    else:
                        p.wait("tensor", sV, ev0 + 1 + (it - 2) - 4 + 1)
                for j in range(NC):
                    p.run("tensor", lambda e, j=j, g=g, bank=bank, wsl=wsl:
                          e.matmul(
                        bank[0:64, 0:512], wsl[:, ts(j, 64)],
                        xn_bf[:, j * T + g * QG:j * T + g * QG + QG],
                        start=(j == 0), stop=(j == NC - 1)),
                        inc=sT if j == NC - 1 else None)
                if which == 0:
                    p.wait("scalar", sT)
                    p.run("scalar", lambda e, g=g, bank=bank: e.activation(
                        q2[0:64, ts(g, QG)], bank[0:64, 0:512], AF.Identity,
                        bias=stat[0:64, 23:24], scale=consts[0:64, 5:6]),
                        inc=sS)
                elif which == 1:
                    p.wait("vector", sT)
                    p.run("vector", lambda e, g=g, bank=bank: e.tensor_scalar(
                        kvstack[0:64, ts(g, QG)], bank[0:64, 0:512],
                        stat[0:64, 21:22], None, ALU.add), inc=sV)
                else:
                    p.wait("vector", sT)
                    p.run("vector", lambda e, g=g, bank=bank: e.tensor_scalar(
                        vloc[0:64, ts(g, QG)], bank[0:64, 0:512],
                        stat[0:64, 22:23], None, ALU.add), inc=sV)
                it += 1
        p.run("vector", lambda e: e.tensor_copy(
            kvstack[64:128, 0:T], vloc[0:64, 0:T]))
        p.run("vector", lambda e: e.tensor_copy(
            q2[64:128, 0:T], q2[0:64, 0:T]), inc=sV)
        p.wait("gpsimd", sV)
        p.wait("gpsimd", sS)
        p.dma("gpsimd", dram["ag_kv_in"], kvstack[:, 0:T], sG)
        p.wait("gpsimd", sG)
        p.run("gpsimd", lambda e: e.collective_compute(
            "AllGather", ALU.bypass, replica_groups=[CORES],
            ins=[ag_kv_in.ap().opt()], outs=[ag_kv_out.ap().opt()]), inc=sE)
        p.wait("gpsimd", sE)
        for c in range(NCO):
            b = c % 2
            p.dma("gpsimd", k_sb[64 * b:64 * b + 64, (c // 2) * T:(c // 2 + 1) * T],
                  dram["ag_kv_out"][128 * c:128 * c + 64, :], sG)
            p.dma("gpsimd", v_sb[64 * b:64 * b + 64, (c // 2) * T:(c // 2 + 1) * T],
                  dram["ag_kv_out"][128 * c + 64:128 * c + 128, :], sG)
        p.run("gpsimd", lambda e: e.memset(
            Vtok[:].rearrange("p (a b) -> p a b", b=65)[:, :, 64:65], 1.0),
            inc=sE)
        p.wait("tensor", sG)
        p.wait("tensor", sE)
        p.wait("tensor", sV)
        ev0 = sV.v
        for kk in range(NKK):
            c = kk // 16
            b = c % 2
            tl = kk % 16
            bank = pb[6 + kk % 2]
            bb = bank[:].bitcast(BF16)
            if kk >= 2:
                p.wait("tensor", sV, ev0 + kk - 1)
            p.run("tensor", lambda e, c=c, b=b, tl=tl, bb=bb: e.transpose(
                bb[:, 0:64],
                v_sb[64 * b:64 * b + 64, (c // 2) * T + tl * 128:(c // 2) * T + tl * 128 + 128],
                identb[64 * b:64 * b + 64, 64 * b:64 * b + 64]), inc=sT)
            p.wait("vector", sT)
            p.run("vector", lambda e, kk=kk, bb=bb: e.tensor_copy(
                Vtok[:, 65 * kk:65 * kk + 64], bb[:, 0:64]), inc=sV)
        p.wait("tensor", sV)
        for g in range(NQG):
            free_at = []
            av_done = []
            for kk in range(NKK):
                c = kk // 16
                b = c % 2
                tl = kk % 16
                bank = pb[kk % 2]
                pp = Pp[kk % 2]
                if kk >= 2:
                    sem, val = free_at[kk - 2]
                    p.wait("tensor", sem, val)
                p.run("tensor", lambda e, c=c, b=b, tl=tl, g=g, bank=bank:
                      e.matmul(
                    bank[:, 0:512],
                    k_sb[64 * b:64 * b + 64, (c // 2) * T + tl * 128:(c // 2) * T + tl * 128 + 128],
                    q2[64 * b:64 * b + 64, ts(g, QG)],
                    start=True, stop=True), inc=sT)
                if kk >= 2:
                    p.wait("scalar", sT, av_done[kk - 2])
                p.wait("scalar", sT)
                p.run("scalar", lambda e, pp=pp, bank=bank: e.activation(
                    pp[:, 0:512], bank[:, 0:512], AF.Exp), inc=sS)
                free_at.append((sS, sS.v))
                p.wait("tensor", sS)
                p.run("tensor", lambda e, kk=kk, pp=pp: e.matmul(
                    pb[2][0:65, 0:512], Vtok[:, 65 * kk:65 * kk + 65],
                    pp[:, 0:512], start=(kk == 0), stop=(kk == NKK - 1)),
                    inc=sT)
                av_done.append(sT.v)
            p.wait("vector", sT)
            p.run("vector", lambda e: e.tensor_copy(
                rows[0:1, 0:512], pb[2][64:65, 0:512]))
            p.run("vector", lambda e: e.drain())
            p.run("vector", lambda e: e.reciprocal(
                rows[0:1, 512:1024], rows[0:1, 0:512]))
            p.run("vector", lambda e: e.drain())
            p.run("vector", lambda e: e.tensor_copy(
                rows_bf[0:1, 0:512], rows[0:1, 512:1024]), inc=sV)
            p.wait("tensor", sV)
            p.run("tensor", lambda e: e.matmul(
                pb[3][:, 0:512], ones_row[:], rows_bf[0:1, 0:512],
                start=True, stop=True), inc=sT)
            p.wait("vector", sT)
            p.run("vector", lambda e: e.tensor_copy(
                rb[0:64, 0:512], pb[3][0:64, 0:512]))
            p.run("vector", lambda e, g=g: e.tensor_tensor(
                y2[0:64, ts(g, QG)], pb[2][0:64, 0:512], rb[0:64, 0:512],
                ALU.mult))
            p.run("vector", lambda e, g=g: e.tensor_copy(
                y2[64:128, ts(g, QG)], y2[0:64, ts(g, QG)]), inc=sV)
            p.wait("tensor", sV)
        for j in range(NC):
            p.run("vector", lambda e, j=j: e.tensor_tensor(
                x_sb[:, ts(j, T)], x_sb[:, ts(j, T)], y2[:, 0:T], ALU.add),
                inc=sV if j == NC - 1 else None)
        p.wait("gpsimd", sV)
        p.wait("scalar", sV)
        p.wait("tensor", sV)
        layer3_ffn()

    def layer3_ffn():
        NCO = len(CORES)
        MSL = T // NCO
        NM = MSL // 128
        NGRP = MSL // 256
        scol = ab(arena2, 32768, F32, 24)
        p.wait("gpsimd", sV)
        p.wait("gpsimd", sS)
        p.wait("gpsimd", sT)
        p.dma("gpsimd", scol[:, 0:24], dram["L3_scols"], sG)
        x1s = ab(arena, 0, BF16, 4 * T)
        x2s = ab(arena, 16384, BF16, 4 * T)
        wsb = [ab(arena, 32768, BF16, 1024), ab(arena, 34816, BF16, 1024)]
        sqb = ab(arena2, 32896, BF16, 512)
        t1 = ab(arena2, 33920, F32, 512)
        for which in range(2):
            wname = ["L3_sw1T", "L3_sw2T"][which]
            dst = [x1s, x2s][which]
            for hc in range(4):
                wb = wsb[hc % 2]
                if which + hc > 0:
                    p.wait("gpsimd", sT)
                chunked_load("gpsimd", wb[:, 0:1024],
                             dram[wname][:, ts(hc, 128)], 128, sG)
                p.wait("tensor", sG)
                esem = sS if which == 0 else sV
                for g in range(NQG):
                    it = hc * NQG + g
                    bank = pb[it % 2]
                    if which + hc + g > 0:
                        p.wait("tensor", esem)
                        p.wait("tensor", sV)
                    for j in range(NC):
                        p.run("tensor", lambda e, j=j, g=g, bank=bank, wb=wb:
                              e.matmul(
                            bank[:, 0:512], wb[:, ts(j, 128)],
                            xn_bf[:, j * T + g * QG:j * T + g * QG + QG],
                            start=(j == 0), stop=(j == NC - 1)),
                            inc=sT if j == NC - 1 else None)
                    if which == 0:
                        p.wait("scalar", sT)
                        p.run("scalar", lambda e, hc=hc, g=g, bank=bank:
                              e.activation(
                            x1s[:, hc * T + g * QG:hc * T + g * QG + QG],
                            bank[:, 0:512], AF.Silu,
                            bias=scol[:, hc:hc + 1]), inc=sS)
                    else:
                        p.wait("vector", sT)
                        p.run("vector", lambda e, hc=hc, g=g, bank=bank:
                              e.tensor_scalar(
                            x2s[:, hc * T + g * QG:hc * T + g * QG + QG],
                            bank[:, 0:512], scol[:, 4 + hc:5 + hc], None,
                            ALU.add), inc=sV)
        p.wait("vector", sS)
        for hc in range(4):
            p.run("vector", lambda e, hc=hc: e.tensor_tensor(
                x1s[:, ts(hc, T)], x1s[:, ts(hc, T)], x2s[:, ts(hc, T)],
                ALU.mult), inc=sV if hc == 3 else None)
        for g in range(NQG):
            p.wait("scalar", sV)
            ss0 = sS.v
            for hc in range(4):
                if hc >= 1:
                    p.wait("scalar", sT)
                p.run("scalar", lambda e, hc=hc, g=g: e.activation(
                    sqb[:, 0:512], x1s[:, hc * T + g * QG:hc * T + g * QG + QG],
                    AF.Square), inc=sS)
                p.wait("tensor", sS)
                p.run("tensor", lambda e, hc=hc, g=g: e.matmul(
                    pb[6][0:1, 0:512], ones_col[:],
                    x1s[:, hc * T + g * QG:hc * T + g * QG + QG],
                    start=(hc == 0), stop=(hc == 3)))
                p.run("tensor", lambda e, hc=hc: e.matmul(
                    pb[7][0:1, 0:512], ones_col[:], sqb[:, 0:512],
                    start=(hc == 0), stop=(hc == 3)),
                    inc=sT)
            p.wait("vector", sT)
            p.run("vector", lambda e: e.tensor_scalar(
                rows[0:1, 0:512], pb[6][0:1, 0:512], consts[0:1, 9:10], None,
                ALU.mult))
            p.run("vector", lambda e: e.tensor_scalar(
                rows[0:1, 512:1024], pb[7][0:1, 0:512], consts[0:1, 9:10],
                None, ALU.mult))
            p.run("vector", lambda e: e.drain())
            p.run("vector", lambda e: e.tensor_tensor(
                t1[0:1, 0:512], rows[0:1, 0:512], rows[0:1, 0:512], ALU.mult))
            p.run("vector", lambda e: e.tensor_tensor(
                rows[0:1, 512:1024], rows[0:1, 512:1024], t1[0:1, 0:512],
                ALU.subtract), inc=sV)
            p.wait("scalar", sV)
            p.run("scalar", lambda e: e.activation(
                rows[0:1, 512:1024], rows[0:1, 512:1024], AF.Sqrt,
                bias=consts[0:1, 3:4]), inc=sS)
            p.wait("vector", sS)
            p.run("vector", lambda e: e.reciprocal(
                rows[0:1, 512:1024], rows[0:1, 512:1024]))
            p.run("vector", lambda e: e.drain())
            p.run("vector", lambda e: e.tensor_copy(
                rows_bf[0:1, 0:512], rows[0:1, 512:1024]))
            p.run("vector", lambda e: e.tensor_tensor(
                rows[0:1, 0:512], rows[0:1, 0:512], rows[0:1, 512:1024],
                ALU.mult))
            p.run("vector", lambda e: e.drain())
            p.run("vector", lambda e: e.tensor_copy(
                rows_bf[0:1, 512:1024], rows[0:1, 0:512]), inc=sV)
            p.wait("tensor", sV)
            p.run("tensor", lambda e: e.matmul(
                pb[4][:, 0:512], ones_row[:], rows_bf[0:1, 0:512],
                start=True, stop=True))
            p.run("tensor", lambda e: e.matmul(
                pb[5][:, 0:512], ones_row[:], rows_bf[0:1, 512:1024],
                start=True, stop=True), inc=sT)
            p.wait("vector", sT)
            for hc in range(4):
                p.run("vector", lambda e, hc=hc, g=g: e.tensor_tensor(
                    t1[:, 0:512],
                    x1s[:, hc * T + g * QG:hc * T + g * QG + QG],
                    pb[4][:, 0:512], ALU.mult))
                p.run("vector", lambda e, hc=hc: e.tensor_tensor(
                    t1[:, 0:512], t1[:, 0:512], pb[5][:, 0:512],
                    ALU.subtract))
                p.run("vector", lambda e, hc=hc, g=g: e.scalar_tensor_tensor(
                    x1s[:, hc * T + g * QG:hc * T + g * QG + QG],
                    t1[:, 0:512], scol[:, 8 + hc:9 + hc],
                    scol[:, 12 + hc:13 + hc].broadcast_to((128, 512)),
                    ALU.mult, ALU.add), inc=sV if hc == 3 else None)
            p.wait("tensor", sV)
        shared_o = ab(arena2, 0, BF16, 8 * T)
        for oc in range(NC):
            wb = wsb[oc % 2]
            p.wait("gpsimd", sT)
            chunked_load("gpsimd", wb[:, 0:512],
                         dram["L3_sw3T"][:, ts(oc, 128)], 128, sG)
            p.wait("tensor", sG)
            for g in range(NQG):
                bank = pb[(oc * NQG + g) % 2]
                p.wait("tensor", sV)
                for j in range(4):
                    p.run("tensor", lambda e, j=j, g=g, bank=bank, wb=wb:
                          e.matmul(
                        bank[:, 0:512], wb[:, ts(j, 128)],
                        x1s[:, j * T + g * QG:j * T + g * QG + QG],
                        start=(j == 0), stop=(j == 3)),
                        inc=sT if j == 3 else None)
                p.wait("vector", sT)
                p.run("vector", lambda e, oc=oc, g=g, bank=bank: e.tensor_scalar(
                    shared_o[:, oc * T + g * QG:oc * T + g * QG + QG],
                    bank[:, 0:512], scol[:, 16 + oc:17 + oc], None, ALU.add),
                    inc=sV)
        p.wait("tensor", sV)
        if L3_PH < 2:
            return
        gcb = [ab(arena, 0, BF16, 2048), ab(arena, 4096, BF16, 2048)]
        selbuf = ab(arena, 8192, F32, 256)
        selc = ab(arena, 9216, F32, 16)
        selstg = ab(arena, 9280, F32, 128)
        selrow = ab(arena, 9792, F32, 2048)
        selrowP = ab(arena, 17984, F32, 2048)
        srow_bf = ab(arena, 26176, BF16, 2048)
        xmc = [ab(arena, 30720, BF16, 1024), ab(arena, 32768, BF16, 1024)]
        for mk in range(16):
            gb = gcb[mk % 2]
            xb = xmc[mk % 2]
            p.wait("gpsimd", sT)
            p.wait("gpsimd", sV)
            p.wait("gpsimd", sW)
            chunked_load("gpsimd", gb[:, 0:2048], dram["Gc"][:, ts(mk, 128)],
                         128, sG)
            p.wait("vector", sG)
            p.wait("tensor", sG)
            for u in range(16):
                p.run("vector", lambda e, u=u, mk=mk, gb=gb: e.tensor_reduce(
                    selbuf[:, u * 16 + mk:u * 16 + mk + 1], gb[:, ts(u, 128)],
                    AX, ALU.add), inc=sV if u == 15 else None)
            for ng in range(2):
                bank = pb[ng % 2]
                p.wait("tensor", sS)
                for u in range(16):
                    p.run("tensor", lambda e, u=u, ng=ng, bank=bank, gb=gb:
                          e.matmul(
                        bank[:, 0:512], gb[:, ts(u, 128)],
                        xn_tok[:, u * D + ng * 512:u * D + ng * 512 + 512],
                        start=(u == 0), stop=(u == 15)),
                        inc=sT if u == 15 else None)
                p.wait("scalar", sT)
                p.wait("scalar", sW, sW.v)
                p.run("scalar", lambda e, ng=ng, bank=bank, xb=xb: e.activation(
                    xb[:, ng * 512:ng * 512 + 512], bank[:, 0:512],
                    AF.Identity), inc=sS)
            p.wait("sync", sS)
            p.wait("sync", sW, sW.v)
            p.dma("sync", dram["xm_in"][ts(mk, 128), :], xb[:, 0:1024], sW)
        p.wait("gpsimd", sW)
        p.wait("gpsimd", sV)
        p.wait("gpsimd", sS)
        p.run("gpsimd", lambda e: e.collective_compute(
            "ReduceScatter", ALU.add, replica_groups=[CORES],
            ins=[xm_in.ap().opt()], outs=[xm_out.ap().opt()]), inc=sE)
        p.run("vector", lambda e: e.drain())
        p.run("vector", lambda e: e.tensor_reduce(
            selc[:, 0:16], selbuf[:, 0:256].rearrange(
                "p (a b) -> p a b", b=16), AX, ALU.add))
        p.run("vector", lambda e: e.drain())
        p.run("vector", lambda e: e.nop(), inc=sV)
        p.wait("tensor", sV)
        p.run("tensor", lambda e: e.transpose(
            pb[3][0:16, 0:128], selc[:, 0:16], identf[:]), inc=sT)
        p.wait("vector", sT)
        p.run("vector", lambda e: e.tensor_copy(
            selstg[0:16, 0:128], pb[3][0:16, 0:128]), inc=sV)
        p.wait("gpsimd", sV)
        p.dma("gpsimd", selrow[0:1, 0:T], selstg[0:16, 0:128], sG)
        p.wait("vector", sG)
        p.run("vector", lambda e: e.tensor_scalar(
            selrowP[0:1, 0:T], selrow[0:1, 0:T], -1.0, None, ALU.mult))
        p.run("vector", lambda e: e.drain())
        p.run("vector", lambda e: e.tensor_scalar(
            selrowP[0:1, 0:T], selrowP[0:1, 0:T], 1.0, None, ALU.add))
        p.run("vector", lambda e: e.drain())
        p.run("vector", lambda e: e.tensor_copy(
            srow_bf[0:1, 0:T], selrowP[0:1, 0:T]), inc=sV)
        for dc in range(NC):
            for tgc in range(NQG):
                p.wait("tensor", sV)
                p.run("tensor", lambda e, tgc=tgc: e.matmul(
                    pb[3][:, 0:512], ones_row[:],
                    srow_bf[0:1, ts(tgc, 512)], start=True, stop=True),
                    inc=sT)
                p.wait("vector", sT)
                p.run("vector", lambda e, dc=dc, tgc=tgc: e.tensor_tensor(
                    t1[:, 0:512],
                    shared_o[:, dc * T + tgc * 512:dc * T + tgc * 512 + 512],
                    pb[3][:, 0:512], ALU.mult))
                p.run("vector", lambda e, dc=dc, tgc=tgc: e.tensor_tensor(
                    x_sb[:, dc * T + tgc * 512:dc * T + tgc * 512 + 512],
                    x_sb[:, dc * T + tgc * 512:dc * T + tgc * 512 + 512],
                    t1[:, 0:512], ALU.add), inc=sV)
        p.wait("tensor", sV)
        if L3_PH < 3:
            return
        layer3_moe(NCO, MSL, NM, NGRP, scol)
        if L3_PH < 4:
            return
        ra = ab(arena, 0, BF16, 16 * 1024)
        p.wait("gpsimd", sE)
        p.wait("gpsimd", sT)
        p.wait("gpsimd", sV)
        chunked_load("gpsimd", ra[:, 0:16 * 1024], dram["rp_out"], 1024, sG)
        g2b = [ab(arena2, 4096 * i, BF16, 2048) for i in range(8)]
        for half in range(2):
            p.wait("gpsimd", sT)
            for mk8 in range(8):
                p.dma("gpsimd", g2b[mk8][:, 0:2048],
                      dram["G2"][ts(half * 8 + mk8, 128), :], sG)
            p.wait("tensor", sG)
            for dc in range(NC):
                for tgc in range(NQG):
                    bank = pb[(dc * NQG + tgc) % 2]
                    p.wait("tensor", sV)
                    for mk8 in range(8):
                        mk = half * 8 + mk8
                        p.run("tensor", lambda e, mk=mk, mk8=mk8, dc=dc,
                              tgc=tgc, bank=bank: e.matmul(
                            bank[:, 0:512],
                            ra[:, mk * 1024 + dc * 128:mk * 1024 + dc * 128 + 128],
                            g2b[mk8][:, ts(tgc, 512)],
                            start=(mk8 == 0), stop=(mk8 == 7)),
                            inc=sT if mk8 == 7 else None)
                    p.wait("vector", sT)
                    p.run("vector", lambda e, dc=dc, tgc=tgc, bank=bank:
                          e.tensor_tensor(
                        x_sb[:, dc * T + tgc * 512:dc * T + tgc * 512 + 512],
                        x_sb[:, dc * T + tgc * 512:dc * T + tgc * 512 + 512],
                        bank[:, 0:512], ALU.add), inc=sV)
        p.wait("tensor", sV)

    def layer3_moe(NCO, MSL, NM, NGRP, scol):
        eb_sb = ab(arena2, 32864, F32, 128)
        sel_sb = ab(arena2, 33376, F32, 8)
        gb_sb = ab(arena2, 33408, F32, 3)
        sg_sb = ab(arena2, 33440, BF16, 8)
        eg_sb = ab(arena2, 33472, BF16, 64)
        esc = ab(arena2, 33600, F32, 256)
        ebuf = ab(arena2, 34624, F32, 256)
        g1 = ab(arena2, 35648, F32, 136)
        we8 = ab(arena2, 0, BF16, 4096)
        wsb2 = [ab(arena2, 8192, BF16, 1024), ab(arena2, 10240, BF16, 1024)]
        t2m = ab(arena2, 12288, F32, 256)
        sq2m = ab(arena2, 13312, BF16, 256)
        x1m = ab(arena2, 13824, BF16, 1024)
        x2m = ab(arena2, 15872, BF16, 1024)
        p.wait("gpsimd", sV)
        p.wait("gpsimd", sS)
        p.dma("gpsimd", eb_sb[:, 0:128], dram["L3_ebcols"], sG)
        p.dma("gpsimd", sel_sb[:, 0:8], dram["L3_selcols"], sG)
        p.dma("gpsimd", gb_sb[:, 0:3], dram["L3_gb"], sG)
        p.dma("gpsimd", sg_sb[:, 0:8], dram["L3_sgwT"], sG)
        chunked_load("gpsimd", eg_sb[:, 0:64], dram["L3_egwT"], 8, sG)
        xmt = ab(arena, 0, BF16, 2 * 1024)
        xm_feat = ab(arena, 4096, BF16, 2048)
        h1s = ab(arena, 8192, BF16, 1024)
        h2s = ab(arena, 10240, BF16, 1024)
        zm = ab(arena, 12288, BF16, 1024)
        oeb = ab(arena, 14336, BF16, 256)
        accf = ab(arena, 14848, F32, 2048)
        replf = ab(arena, 23040, BF16, 2048)
        rtok = ab(arena, 27136, BF16, 2048)
        p.wait("gpsimd", sE)
        for grp in range(NGRP):
            p.wait("gpsimd", sT)
            chunked_load("gpsimd", xmt[:, 0:2 * 1024],
                         dram["xm_out"][256 * grp:256 * grp + 256, :], 1024,
                         sG)
            p.wait("tensor", sG)
            p.wait("vector", sG)
            p.wait("scalar", sG)
            ev0 = sV.v
            for i2 in range(16):
                m2, dc = divmod(i2, 8)
                bank = pb[6 + i2 % 2]
                bb = bank[:].bitcast(BF16)
                if i2 >= 2:
                    p.wait("tensor", sV, ev0 + i2 - 1)
                p.run("tensor", lambda e, m2=m2, dc=dc, bb=bb:
                      e.transpose(
                    bb[:, 0:128],
                    xmt[:, m2 * 1024 + dc * 128:m2 * 1024 + dc * 128 + 128],
                    identb[:]), inc=sT)
                p.wait("vector", sT)
                p.run("vector", lambda e, m2=m2, dc=dc, bb=bb: e.tensor_copy(
                    xm_feat[:, dc * 256 + m2 * 128:dc * 256 + m2 * 128 + 128],
                    bb[:, 0:128]), inc=sV)
            p.wait("tensor", sV)
            if L3M < 1:
                continue
            for j in range(NC):
                p.run("tensor", lambda e, j=j: e.matmul(
                    pb[4][0:1, 0:256], sg_sb[:, j:j + 1],
                    xm_feat[:, ts(j, 256)], start=(j == 0), stop=(j == NC - 1)))
                p.run("tensor", lambda e, j=j: e.matmul(
                    pb[5][0:8, 0:256], eg_sb[:, ts(j, 8)],
                    xm_feat[:, ts(j, 256)], start=(j == 0), stop=(j == NC - 1)),
                    inc=sT if j == NC - 1 else None)
            p.wait("scalar", sT)
            p.run("scalar", lambda e: e.activation(
                rows[0:1, 0:256], pb[4][0:1, 0:256], AF.Sigmoid,
                bias=gb_sb[0:1, 2:3]))
            p.run("scalar", lambda e: e.activation(
                esc[0:8, 0:256], pb[5][0:8, 0:256], AF.Sigmoid,
                bias=gb_sb[0:8, 0:1]), inc=sS)
            p.wait("vector", sS)
            p.run("vector", lambda e: e.tensor_scalar(
                ebuf[0:8, 0:256], esc[0:8, 0:256], gb_sb[0:8, 1:2], None,
                ALU.add), inc=sV)
            if L3M < 2:
                continue
            for c2 in range(2):
                p.wait("tensor", sV)
                p.run("tensor", lambda e, c2=c2: e.transpose(
                    pb[6][0:128, 0:8], ebuf[0:8, ts(c2, 128)],
                    identf[0:8, 0:8]))
                p.run("tensor", lambda e, c2=c2: e.transpose(
                    pb[7][0:128, 0:8], esc[0:8, ts(c2, 128)],
                    identf[0:8, 0:8]))
                p.run("tensor", lambda e, c2=c2: e.transpose(
                    pb[2][0:128, 0:1], rows[0:1, ts(c2, 128)],
                    identf[0:1, 0:1]), inc=sT)
                p.wait("vector", sT)
                p.run("vector", lambda e, c2=c2: e.tensor_copy(
                    g1[:, 8 * c2:8 + 8 * c2], pb[6][0:128, 0:8]))
                p.run("vector", lambda e, c2=c2: e.tensor_copy(
                    g1[:, 16 + 8 * c2:24 + 8 * c2], pb[7][0:128, 0:8]))
                p.run("vector", lambda e, c2=c2: e.tensor_copy(
                    g1[:, 52 + c2:53 + c2], pb[2][0:128, 0:1]))
                if L3G < 2:
                    continue
                p.run("vector", lambda e: e.drain())
                p.run("vector", lambda e, c2=c2: e.tensor_reduce(
                    g1[:, 48 + c2:49 + c2], g1[:, 8 * c2:8 * c2 + 8], AX,
                    ALU.max))
                p.run("vector", lambda e: e.drain())
                p.run("vector", lambda e, c2=c2: e.tensor_scalar(
                    g1[:, 32 + 8 * c2:40 + 8 * c2], g1[:, 8 * c2:8 * c2 + 8],
                    g1[:, 48 + c2:49 + c2], None, ALU.is_equal))
                p.run("vector", lambda e: e.drain())
                p.run("vector", lambda e, c2=c2: e.tensor_tensor(
                    g1[:, 74 + 8 * c2:82 + 8 * c2],
                    g1[:, 32 + 8 * c2:40 + 8 * c2],
                    g1[:, 16 + 8 * c2:24 + 8 * c2], ALU.mult))
                p.run("vector", lambda e: e.drain())
                p.run("vector", lambda e, c2=c2: e.tensor_reduce(
                    g1[:, 50 + c2:51 + c2], g1[:, 74 + 8 * c2:82 + 8 * c2],
                    AX, ALU.add))
                if L3G < 3:
                    continue
                p.run("vector", lambda e: e.drain())
                p.run("vector", lambda e, c2=c2: e.tensor_tensor(
                    g1[:, 54 + c2:55 + c2], g1[:, 52 + c2:53 + c2],
                    g1[:, 50 + c2:51 + c2], ALU.subtract))
                p.run("vector", lambda e: e.drain())
                p.run("vector", lambda e: e.nop(), inc=sV)
                p.wait("scalar", sV)
                p.run("scalar", lambda e, c2=c2: e.activation(
                    g1[:, 56 + c2:57 + c2], g1[:, 54 + c2:55 + c2],
                    AF.Sigmoid), inc=sS)
                if L3G < 4:
                    continue
                p.wait("vector", sS)
                p.run("vector", lambda e, c2=c2: e.tensor_scalar(
                    g1[:, 58 + c2:59 + c2], g1[:, 56 + c2:57 + c2], -1.0,
                    None, ALU.mult))
                p.run("vector", lambda e: e.drain())
                p.run("vector", lambda e, c2=c2: e.tensor_scalar(
                    g1[:, 60 + c2:61 + c2], g1[:, 58 + c2:59 + c2], 1.0,
                    None, ALU.add))
                p.run("vector", lambda e: e.drain())
                p.run("vector", lambda e, c2=c2: e.tensor_scalar(
                    g1[:, 100 + 8 * c2:108 + 8 * c2],
                    g1[:, 32 + 8 * c2:40 + 8 * c2],
                    g1[:, 60 + c2:61 + c2], None, ALU.mult))
                p.run("vector", lambda e: e.drain())
                p.run("vector", lambda e: e.nop(), inc=sV)
            if L3G < 5:
                continue
            for c2 in range(2):
                p.wait("tensor", sV)
                p.run("tensor", lambda e, c2=c2: e.transpose(
                    pb[6][0:8, 0:128], g1[:, 100 + 8 * c2:108 + 8 * c2],
                    identf[:]))
                p.run("tensor", lambda e, c2=c2: e.transpose(
                    pb[7][0:1, 0:128], g1[:, 56 + c2:57 + c2], identf[:]),
                    inc=sT)
                p.wait("vector", sT)
                p.run("vector", lambda e, c2=c2: e.tensor_copy(
                    ebuf[0:8, ts(c2, 128)], pb[6][0:8, 0:128]))
                p.run("vector", lambda e, c2=c2: e.tensor_copy(
                    rows[0:1, 256 + 128 * c2:256 + 128 * c2 + 128],
                    pb[7][0:1, 0:128]), inc=sV)
            p.run("vector", lambda e: e.drain())
            p.run("vector", lambda e: e.tensor_copy(
                rows_bf[0:1, 0:256], rows[0:1, 256:512]), inc=sV)
            if L3M < 3:
                continue
            for which in range(2):
                wname = ["L3_sw1T", "L3_sw2T"][which]
                for hc in range(4):
                    wb = wsb2[hc % 2]
                    p.wait("gpsimd", sT)
                    chunked_load("gpsimd", wb[:, 0:1024],
                                 dram[wname][:, ts(hc, 128)], 128, sG)
                    p.wait("tensor", sG)
                    p.wait("tensor", sS)
                    p.wait("tensor", sV)
                    bank = pb[hc % 2]
                    for j in range(NC):
                        p.run("tensor", lambda e, j=j, bank=bank, wb=wb:
                              e.matmul(
                            bank[:, 0:256], wb[:, ts(j, 128)],
                            xm_feat[:, ts(j, 256)],
                            start=(j == 0), stop=(j == NC - 1)),
                            inc=sT if j == NC - 1 else None)
                    if which == 0:
                        p.wait("scalar", sT)
                        p.run("scalar", lambda e, hc=hc, bank=bank:
                              e.activation(
                            x1m[:, ts(hc, 256)], bank[:, 0:256], AF.Silu,
                            bias=scol[:, hc:hc + 1]), inc=sS)
                    else:
                        p.wait("vector", sT)
                        p.run("vector", lambda e, hc=hc, bank=bank:
                              e.tensor_scalar(
                            x2m[:, ts(hc, 256)], bank[:, 0:256],
                            scol[:, 4 + hc:5 + hc], None, ALU.add), inc=sV)
            p.wait("vector", sS)
            for hc in range(4):
                p.run("vector", lambda e, hc=hc: e.tensor_tensor(
                    x1m[:, ts(hc, 256)], x1m[:, ts(hc, 256)],
                    x2m[:, ts(hc, 256)], ALU.mult),
                    inc=sV if hc == 3 else None)
            ln_apply_256(x1m, x1m, scol, 8, 12)
            for oc in range(NC):
                wb = wsb2[oc % 2]
                p.wait("gpsimd", sT)
                chunked_load("gpsimd", wb[:, 0:512],
                             dram["L3_sw3T"][:, ts(oc, 128)], 128, sG)
                p.wait("tensor", sG)
                p.wait("tensor", sV)
                bank = pb[oc % 2]
                for j in range(4):
                    p.run("tensor", lambda e, j=j, bank=bank, wb=wb: e.matmul(
                        bank[:, 0:256], wb[:, ts(j, 128)],
                        x1m[:, ts(j, 256)], start=(j == 0), stop=(j == 3)),
                        inc=sT if j == 3 else None)
                p.wait("vector", sT)
                p.run("vector", lambda e, oc=oc, bank=bank: e.tensor_scalar(
                    rtok[:, ts(oc, 256)], bank[:, 0:256],
                    scol[:, 16 + oc:17 + oc], None, ALU.add), inc=sV)
            p.wait("tensor", sV)
            if L3M < 4:
                continue
            for e_ in range(8):
                p.wait("gpsimd", sT)
                p.wait("gpsimd", sV)
                chunked_load("gpsimd", we8[:, 0:4096],
                             dram["L3_ew1T"][1024 * e_:1024 * e_ + 1024, :],
                             512, sG)
                p.wait("tensor", sG)
                p.wait("tensor", sS)
                for hc in range(4):
                    bank = pb[hc % 2]
                    if hc >= 2:
                        p.wait("tensor", sS)
                    for j in range(NC):
                        p.run("tensor", lambda e, j=j, hc=hc, bank=bank:
                              e.matmul(
                            bank[:, 0:256], we8[:, j * 512 + hc * 128:j * 512 + hc * 128 + 128],
                            xm_feat[:, ts(j, 256)],
                            start=(j == 0), stop=(j == NC - 1)),
                            inc=sT if j == NC - 1 else None)
                    p.wait("scalar", sT)
                    p.run("scalar", lambda e, hc=hc, e_=e_, bank=bank:
                          e.activation(
                        h1s[:, ts(hc, 256)], bank[:, 0:256], AF.Silu,
                        bias=eb_sb[:, 16 * e_ + hc:16 * e_ + hc + 1]), inc=sS)
                p.wait("gpsimd", sT)
                p.wait("gpsimd", sS)
                chunked_load("gpsimd", we8[:, 0:4096],
                             dram["L3_ew2T"][1024 * e_:1024 * e_ + 1024, :],
                             512, sG)
                p.wait("tensor", sG)
                for hc in range(4):
                    bank = pb[hc % 2]
                    p.wait("tensor", sV)
                    for j in range(NC):
                        p.run("tensor", lambda e, j=j, hc=hc, bank=bank:
                              e.matmul(
                            bank[:, 0:256], we8[:, j * 512 + hc * 128:j * 512 + hc * 128 + 128],
                            xm_feat[:, ts(j, 256)],
                            start=(j == 0), stop=(j == NC - 1)),
                            inc=sT if j == NC - 1 else None)
                    p.wait("vector", sT)
                    p.run("vector", lambda e, hc=hc, e_=e_, bank=bank:
                          e.tensor_scalar(
                        h2s[:, ts(hc, 256)], bank[:, 0:256],
                        eb_sb[:, 16 * e_ + 4 + hc:16 * e_ + 5 + hc], None,
                        ALU.add), inc=sV)
                p.wait("vector", sS)
                for hc in range(4):
                    p.run("vector", lambda e, hc=hc: e.tensor_tensor(
                        h1s[:, ts(hc, 256)], h1s[:, ts(hc, 256)],
                        h2s[:, ts(hc, 256)], ALU.mult),
                        inc=sV if hc == 3 else None)
                ln_apply_256(h1s, zm, sel_sb, 0, 4)
                p.wait("gpsimd", sT)
                p.wait("gpsimd", sS)
                p.wait("gpsimd", sV)
                chunked_load("gpsimd", we8[:, 0:4096],
                             dram["L3_ew3T"][512 * e_:512 * e_ + 512, :],
                             1024, sG)
                p.dma("gpsimd", rows_bf[0:1, 512:768], ebuf[e_:e_ + 1, 0:256],
                      sG)
                p.wait("tensor", sG)
                p.run("tensor", lambda e: e.matmul(
                    pb[3][:, 0:256], ones_row[:], rows_bf[0:1, 512:768],
                    start=True, stop=True), inc=sT)
                p.wait("tensor", sG)
                for oc in range(NC):
                    bank = pb[oc % 2]
                    p.wait("tensor", sV)
                    for j in range(4):
                        p.run("tensor", lambda e, j=j, oc=oc, bank=bank:
                              e.matmul(
                            bank[:, 0:256], we8[:, j * 1024 + oc * 128:j * 1024 + oc * 128 + 128],
                            zm[:, ts(j, 256)], start=(j == 0), stop=(j == 3)),
                            inc=sT if j == 3 else None)
                    p.wait("vector", sT)
                    p.run("vector", lambda e, oc=oc, e_=e_, bank=bank:
                          e.tensor_scalar(
                        oeb[:, 0:256], bank[:, 0:256],
                        eb_sb[:, 16 * e_ + 8 + oc:16 * e_ + 9 + oc], None,
                        ALU.add))
                    p.run("vector", lambda e: e.drain())
                    if e_ == 0:
                        p.run("vector", lambda e, oc=oc: e.tensor_tensor(
                            accf[:, ts(oc, 256)].bitcast(F32)[:, 0:256] if False
                            else accf[:, oc * 256:oc * 256 + 256],
                            oeb[:, 0:256], pb[3][:, 0:256], ALU.mult), inc=sV)
                    else:
                        p.run("vector", lambda e, oc=oc: e.tensor_tensor(
                            t2m[:, 0:256], oeb[:, 0:256], pb[3][:, 0:256],
                            ALU.mult))
                        p.run("vector", lambda e: e.drain())
                        p.run("vector", lambda e, oc=oc: e.tensor_tensor(
                            accf[:, oc * 256:oc * 256 + 256],
                            accf[:, oc * 256:oc * 256 + 256],
                            t2m[:, 0:256], ALU.add), inc=sV)
                p.wait("tensor", sV)
            if L3M < 5:
                continue
            p.run("vector", lambda e: e.tensor_copy(
                rows_bf[0:1, 512:768], rows[0:1, 256:512]), inc=sV)
            p.wait("tensor", sV)
            p.run("tensor", lambda e: e.matmul(
                pb[3][:, 0:256], ones_row[:], rows_bf[0:1, 512:768],
                start=True, stop=True), inc=sT)
            p.wait("vector", sT)
            for oc in range(NC):
                p.run("vector", lambda e, oc=oc: e.tensor_tensor(
                    t2m[:, 0:256], rtok[:, ts(oc, 256)], pb[3][:, 0:256],
                    ALU.mult))
                p.run("vector", lambda e: e.drain())
                p.run("vector", lambda e, oc=oc: e.tensor_tensor(
                    t2m[:, 0:256], t2m[:, 0:256],
                    accf[:, oc * 256:oc * 256 + 256], ALU.add))
                p.run("vector", lambda e: e.drain())
                p.run("vector", lambda e, oc=oc: e.tensor_copy(
                    replf[:, ts(oc, 256)], t2m[:, 0:256]), inc=sV)
            p.wait("tensor", sV)
            ev0 = sV.v
            for i2 in range(16):
                dc, m2 = divmod(i2, 2)
                bank = pb[6 + i2 % 2]
                bb = bank[:].bitcast(BF16)
                if i2 >= 2:
                    p.wait("tensor", sV, ev0 + i2 - 1)
                p.run("tensor", lambda e, dc=dc, m2=m2, bb=bb: e.transpose(
                    bb[:, 0:128],
                    replf[:, dc * 256 + m2 * 128:dc * 256 + m2 * 128 + 128],
                    identb[:]), inc=sT)
                p.wait("vector", sT)
                p.run("vector", lambda e, dc=dc, m2=m2, bb=bb: e.tensor_copy(
                    rtok2_view(rtok, m2, dc), bb[:, 0:128]), inc=sV)
            p.wait("sync", sV)
            for m2 in range(2):
                p.dma("sync", dram["rp_in"][grp * 256 + m2 * 128:grp * 256 + m2 * 128 + 128, :],
                      rtok[:, m2 * 1024:m2 * 1024 + 1024], sW)
        p.wait("gpsimd", sW)
        p.run("gpsimd", lambda e: e.collective_compute(
            "AllGather", ALU.bypass, replica_groups=[CORES],
            ins=[rp_in.ap().opt()], outs=[rp_out.ap().opt()]), inc=sE)

    def rtok2_view(rtok, m2, dc):
        return rtok[:, m2 * 1024 + dc * 128:m2 * 1024 + dc * 128 + 128]

    def ln_apply_256(src, dst, gtab, gcol0, bcol0):
        sq2m = ab(arena2, 13312, BF16, 256)
        t2m = ab(arena2, 12288, F32, 256)
        p.wait("scalar", sV)
        for hc in range(4):
            if hc >= 1:
                p.wait("scalar", sT)
            p.run("scalar", lambda e, hc=hc: e.activation(
                sq2m[:, 0:256], src[:, ts(hc, 256)], AF.Square), inc=sS)
            p.wait("tensor", sS)
            p.run("tensor", lambda e, hc=hc: e.matmul(
                pb[6][0:1, 0:256], ones_col[:], src[:, ts(hc, 256)],
                start=(hc == 0), stop=(hc == 3)))
            p.run("tensor", lambda e, hc=hc: e.matmul(
                pb[7][0:1, 0:256], ones_col[:], sq2m[:, 0:256],
                start=(hc == 0), stop=(hc == 3)), inc=sT)
        p.wait("vector", sT)
        p.run("vector", lambda e: e.tensor_scalar(
            rows[0:1, 512:768], pb[6][0:1, 0:256], consts[0:1, 9:10], None,
            ALU.mult))
        p.run("vector", lambda e: e.tensor_scalar(
            rows[0:1, 768:1024], pb[7][0:1, 0:256], consts[0:1, 9:10], None,
            ALU.mult))
        p.run("vector", lambda e: e.drain())
        p.run("vector", lambda e: e.tensor_tensor(
            t2m[0:1, 0:256], rows[0:1, 512:768], rows[0:1, 512:768],
            ALU.mult))
        p.run("vector", lambda e: e.drain())
        p.run("vector", lambda e: e.tensor_tensor(
            rows[0:1, 768:1024], rows[0:1, 768:1024], t2m[0:1, 0:256],
            ALU.subtract), inc=sV)
        p.wait("scalar", sV)
        p.run("scalar", lambda e: e.activation(
            rows[0:1, 768:1024], rows[0:1, 768:1024], AF.Sqrt,
            bias=consts[0:1, 3:4]), inc=sS)
        p.wait("vector", sS)
        p.run("vector", lambda e: e.reciprocal(
            rows[0:1, 768:1024], rows[0:1, 768:1024]))
        p.run("vector", lambda e: e.drain())
        p.run("vector", lambda e: e.tensor_copy(
            rows_bf[0:1, 768:1024], rows[0:1, 768:1024]))
        p.run("vector", lambda e: e.tensor_tensor(
            rows[0:1, 512:768], rows[0:1, 512:768], rows[0:1, 768:1024],
            ALU.mult))
        p.run("vector", lambda e: e.drain())
        p.run("vector", lambda e: e.tensor_copy(
            rows_bf[0:1, 256:512], rows[0:1, 512:768]), inc=sV)
        p.wait("tensor", sV)
        p.run("tensor", lambda e: e.matmul(
            pb[4][:, 0:256], ones_row[:], rows_bf[0:1, 768:1024],
            start=True, stop=True))
        p.run("tensor", lambda e: e.matmul(
            pb[5][:, 0:256], ones_row[:], rows_bf[0:1, 256:512],
            start=True, stop=True), inc=sT)
        p.wait("vector", sT)
        for hc in range(4):
            p.run("vector", lambda e, hc=hc: e.tensor_tensor(
                t2m[:, 0:256], src[:, ts(hc, 256)], pb[4][:, 0:256],
                ALU.mult))
            p.run("vector", lambda e: e.drain())
            p.run("vector", lambda e, hc=hc: e.tensor_tensor(
                t2m[:, 0:256], t2m[:, 0:256], pb[5][:, 0:256], ALU.subtract))
            p.run("vector", lambda e: e.drain())
            p.run("vector", lambda e, hc=hc: e.scalar_tensor_tensor(
                dst[:, ts(hc, 256)], t2m[:, 0:256],
                gtab[:, gcol0 + hc:gcol0 + hc + 1],
                gtab[:, bcol0 + hc:bcol0 + hc + 1].broadcast_to((128, 256)),
                ALU.mult, ALU.add), inc=sV if hc == 3 else None)
        p.wait("tensor", sV)

    if N_LAYERS >= 1:
        layer0()
    if N_LAYERS >= 2:
        layer1()
    if N_LAYERS >= 3:
        layer2()
    if N_LAYERS >= 4:
        layer3()

    if DBG_REGION is not None:
        which, off, n, dt_ = DBG_REGION
        src_t = arena if which == "arena" else arena2
        dt2 = BF16 if dt_ == "bf16" else F16
        view = ab(src_t, off, dt2, n)
        stg = ab(arena if which == "arena2" else arena2, 0, F32, 2048)
        p.wait("vector", sS)
        p.wait("vector", sT)
        p.wait("vector", sG)
        nch = (n + 2047) // 2048
        for c in range(nch):
            w = min(2048, n - c * 2048)
            p.run("vector", lambda e, c=c, w=w, view=view, stg=stg: e.tensor_copy(
                stg[:, 0:w], view[:, c * 2048:c * 2048 + w]), inc=sV)
            p.wait("vector", sV)
            p.run("vector", lambda e, c=c, w=w, stg=stg: e.tensor_copy(
                x_sb[:, c * 2048:c * 2048 + w] if False else
                x_sb[:, 0:0 + w], stg[:, 0:w]), inc=sV) if False else None
            p.run("vector", lambda e, c=c, w=w, stg=stg: e.tensor_copy(
                x_sb[:, ts(c, 2048)][:, 0:w], stg[:, 0:w]), inc=sV)
            p.wait("vector", sV)

    p.wait("sync", sV)
    p.wait("sync", sS)
    p.wait("sync", sT)
    for j in range(NC):
        p.dma("sync", out_ap[ts(j, 128), :], x_sb[:, ts(j, T)], sW)
    p.wait("sync", sW)

    block = st.enter_context(nc.Block())
    p.realize(block)
    st.close()
    return nc


def host_inputs(inp):
    c = np.ascontiguousarray
    f32 = np.float32
    d = {}
    d["L0_WqA"] = c(np.concatenate([np.asarray(inp["L0_attn_Wq"], f32),
                                    np.asarray(inp["L0_attn_bq"], f32)[:, None]], 1))
    d["L0_WkA"] = c(np.concatenate([np.asarray(inp["L0_attn_Wk"], f32),
                                    np.asarray(inp["L0_attn_bk"], f32)[:, None]], 1))
    d["L0_W"] = c(np.asarray(inp["L0_ffn_weight"], f32))
    d["L0_A1"] = c(np.asarray(inp["L0_ffn_aux_weight1"], f32))
    d["L0_A2"] = c(np.asarray(inp["L0_ffn_aux_weight2"], f32))
    d["L0_bias"] = c(np.asarray(inp["L0_ffn_bias"], f32).reshape(8, 128).T)
    d["L0_bqc"] = c(np.asarray(inp["L0_attn_bq"], f32).reshape(8, 128).T)
    d["L1_WattnT"] = c(np.asarray(inp["L1_attn_W_attn"], f32).T)
    d["L1_WprojT"] = c(np.asarray(inp["L1_attn_W_proj"], f32).T)
    d["L1_wgT"] = c(np.asarray(inp["L1_ffn_w_gate"], f32).T)
    d["L1_wuT"] = c(np.asarray(inp["L1_ffn_w_up"], f32).T)
    d["L1_wdT"] = c(np.asarray(inp["L1_ffn_w_down"], f32).T)
    d["L3_WqT"] = c(np.asarray(inp["L3_attn_Wq"], f32).T)
    d["L3_WkT"] = c(np.asarray(inp["L3_attn_Wk"], f32).T)
    d["L3_WvT"] = c(np.asarray(inp["L3_attn_Wv"], f32).T)
    d["L3_bqkv"] = c(np.stack([np.asarray(inp["L3_attn_bq"], f32),
                               np.asarray(inp["L3_attn_bk"], f32),
                               np.asarray(inp["L3_attn_bv"], f32)], 1))
    d["L3_sw1T"] = c(np.asarray(inp["L3_ffn_sw1"], f32).T)
    d["L3_sw2T"] = c(np.asarray(inp["L3_ffn_sw2"], f32).T)
    d["L3_sw3T"] = c(np.asarray(inp["L3_ffn_sw3"], f32).T)

    def cols(v, n):
        return np.asarray(v, f32).reshape(n, 128).T
    d["L3_scols"] = c(np.concatenate(
        [cols(inp["L3_ffn_sb1"], 4), cols(inp["L3_ffn_sb2"], 4),
         cols(inp["L3_ffn_s_ln_g"], 4), cols(inp["L3_ffn_s_ln_b"], 4),
         cols(inp["L3_ffn_sb3"], 8)], 1))
    d["L3_sgwT"] = c(np.asarray(inp["L3_ffn_sg_w"], f32).reshape(8, 128).T)
    d["L3_egwT"] = c(np.asarray(inp["L3_ffn_eg_w"], f32).T)
    gb = np.zeros((128, 3), f32)
    gb[0:8, 0] = np.asarray(inp["L3_ffn_eg_b"], f32)
    gb[0:8, 1] = np.asarray(inp["L3_ffn_exp_bias"], f32)
    gb[0:1, 2] = np.asarray(inp["L3_ffn_sg_b"], f32)
    d["L3_gb"] = c(gb)
    d["L3_ew1T"] = c(np.concatenate(
        [np.asarray(inp["L3_ffn_ew1"], f32)[e].T for e in range(8)], 0))
    d["L3_ew2T"] = c(np.concatenate(
        [np.asarray(inp["L3_ffn_ew2"], f32)[e].T for e in range(8)], 0))
    d["L3_ew3T"] = c(np.concatenate(
        [np.asarray(inp["L3_ffn_ew3"], f32)[e].T for e in range(8)], 0))
    ebc = np.zeros((128, 128), f32)
    for e in range(8):
        ebc[:, 16 * e:16 * e + 4] = cols(np.asarray(inp["L3_ffn_eb1"], f32)[e], 4)
        ebc[:, 16 * e + 4:16 * e + 8] = cols(np.asarray(inp["L3_ffn_eb2"], f32)[e], 4)
        ebc[:, 16 * e + 8:16 * e + 16] = cols(np.asarray(inp["L3_ffn_eb3"], f32)[e], 8)
    d["L3_ebcols"] = c(ebc)
    d["L3_selcols"] = c(np.concatenate(
        [cols(inp["L3_ffn_sel_ln_g"], 4), cols(inp["L3_ffn_sel_ln_b"], 4)], 1))
    d["L2_Wq"] = c(np.asarray(inp["L2_attn_Wq"], f32))
    d["L2_Wk"] = c(np.asarray(inp["L2_attn_Wk"], f32))
    d["L2_Wv"] = c(np.asarray(inp["L2_attn_Wv"], f32))
    d["L2_Wo"] = c(np.asarray(inp["L2_attn_Wo"], f32))
    d["L2_Wdkv"] = c(np.asarray(inp["L2_attn_W_down_kv"], f32))
    d["L2_Wupk"] = c(np.asarray(inp["L2_attn_W_up_k"], f32))
    d["L2_Wupv"] = c(np.asarray(inp["L2_attn_W_up_v"], f32))
    d["L2_wgN"] = c(np.asarray(inp["L2_ffn_w_gate"], f32))
    d["L2_wuN"] = c(np.asarray(inp["L2_ffn_w_up"], f32))
    d["L2_wdN"] = c(np.asarray(inp["L2_ffn_w_down"], f32))
    d["L2_wgT"] = c(d["L2_wgN"].T)
    d["L2_wuT"] = c(d["L2_wuN"].T)
    d["L2_wdT"] = c(d["L2_wdN"].T)
    bdj = np.zeros((24, 128, 128), f32)
    for j in range(24):
        for pp_ in range(128):
            r = 4 * j + pp_ // 32
            if r < 128:
                bdj[j, r, pp_] = 1.0
    d["BDJ"] = c(bdj.transpose(1, 0, 2).reshape(128, 24 * 128))
    d["IOTA_T"] = c(np.tile(np.arange(2048, dtype=f32), (128, 1)))
    ridx = np.full((8, 128), 1e9, f32)
    Lv, base = 4, 0
    sets = []
    while Lv <= 2048:
        for i in range(2048 // Lv):
            sets.append((i + 1) * Lv - 1)
        Lv *= 2
    sets = np.asarray(sets, f32)
    ridx.reshape(-1)[:1023] = sets
    d["L1_ridx"] = c(ridx.reshape(8, 128).T)
    return d


def percore_inputs(inputs, cidx):
    import ml_dtypes
    x = np.asarray(inputs["L0_attn_x"], np.float32)
    ib = np.asarray(inputs["L3_ffn_index_b"]).astype(np.int64)
    it_ = np.asarray(inputs["L3_ffn_index_t"]).astype(np.int64)
    m = {}
    m["xT"] = np.ascontiguousarray(x[cidx].T)
    Gc = np.zeros((2048, 2048), ml_dtypes.bfloat16)
    own = ib == cidx
    Gc[it_[own], np.nonzero(own)[0]] = 1
    m["Gc"] = Gc
    m["G2"] = np.ascontiguousarray(Gc.T)
    return m


def kernel(**inputs):
    shared = host_inputs(inputs)
    in_maps = []
    for cidx in range(8):
        m = dict(shared)
        m.update(percore_inputs(inputs, cidx))
        in_maps.append(m)
    nc = build_nc()
    res = run_bass_kernel_spmd(nc, in_maps, core_ids=list(range(8)))
    out = np.stack([np.asarray(r["out"]).T for r in res.results], 0)
    return np.ascontiguousarray(out).astype(np.float32)
